# version 04
# baseline (speedup 1.0000x reference)
import numpy as np
from contextlib import ExitStack

import concourse.bass as bass
import concourse.mybir as mybir
from concourse.bass_utils import run_bass_kernel_spmd

F32 = mybir.dt.float32
BF16 = mybir.dt.bfloat16
AF = mybir.ActivationFunctionType
ALU = mybir.AluOpType
AX = mybir.AxisListType

B, T, D = 16, 2048, 1024
NCORES = 8
SPC = B // NCORES
H, DK = 16, 64
NP = H // 2
EC = D // 128
TTILES = T // 128
QCH = 512
NQC = T // QCH
FNH, RAILS, RW, FMULT = 4, 64, 16, 4
F_TOT = FNH * RAILS * RW * FMULT
SUPF = 2048
NSUP = F_TOT // SUPF
SUPC = SUPF // 128
LN_EPS = 1e-5

ALL_SEMS = ("te", "ve", "act", "pool", "dsp", "dgp", "dsc")


class KB:
    def __init__(self, nc, ctx):
        self.nc = nc
        self.sems, self.cnt = {}, {}
        for name in ALL_SEMS:
            self.sems[name] = ctx.enter_context(nc.semaphore("s_" + name))
            self.cnt[name] = 0
        self.engs = {"te": nc.tensor, "ve": nc.vector, "act": nc.scalar,
                     "pool": nc.gpsimd, "sp": nc.sync}

    def inc(self, name, instr, k=1):
        instr.then_inc(self.sems[name], k)
        self.cnt[name] += k
        return self.cnt[name]

    def dma(self, qname, out, in_, **kw):
        eng = {"dsp": self.nc.sync, "dgp": self.nc.gpsimd, "dsc": self.nc.scalar}[qname]
        return self.inc(qname, eng.dma_start(out=out, in_=in_, **kw), 16)

    def wait(self, eng_name, sem_name, val=None):
        v = self.cnt[sem_name] if val is None else val
        if v > 0:
            self.engs[eng_name].wait_ge(self.sems[sem_name], v)

    def join(self, engines=("te", "ve", "act", "pool", "sp")):
        for e in engines:
            for s in ALL_SEMS:
                self.wait(e, s)

    def fence(self, name, instr):
        m = self.inc(name, instr)
        self.engs[name].wait_ge(self.sems[name], m)
        return m

    def snap(self):
        return dict(self.cnt)


def build_program(stages=("a", "f1", "b", "f2"), n_seq=SPC, dbg=False):
    if dbg is True:
        dbg = "a"
    nc = bass.Bass("TRN2", target_bir_lowering=False, debug=False)

    x_in = nc.dram_tensor("L0_attn_a_x", [n_seq, T, D], F32, kind="ExternalInput").ap()
    wq_a = nc.dram_tensor("L0_attn_a_Wq", [D, D], F32, kind="ExternalInput").ap()
    wk_a = nc.dram_tensor("L0_attn_a_Wk", [D, D], F32, kind="ExternalInput").ap()
    wv_a = nc.dram_tensor("L0_attn_a_Wv", [D, D], F32, kind="ExternalInput").ap()
    wo_a = nc.dram_tensor("L0_attn_a_Wo", [D, D], F32, kind="ExternalInput").ap()
    wq_b = nc.dram_tensor("L0_attn_b_Wq", [H, D, DK], F32, kind="ExternalInput").ap()
    bq_b = nc.dram_tensor("L0_attn_b_bq", [H, DK], F32, kind="ExternalInput").ap()
    wk_b = nc.dram_tensor("L0_attn_b_Wk", [H, D, DK], F32, kind="ExternalInput").ap()
    bk_b = nc.dram_tensor("L0_attn_b_bk", [H, DK], F32, kind="ExternalInput").ap()
    wv_b = nc.dram_tensor("L0_attn_b_Wv", [H, D, DK], F32, kind="ExternalInput").ap()
    bv_b = nc.dram_tensor("L0_attn_b_bv", [H, DK], F32, kind="ExternalInput").ap()
    wo_b = nc.dram_tensor("L0_attn_b_Wo", [D, D], F32, kind="ExternalInput").ap()
    bo_b = nc.dram_tensor("L0_attn_b_bo", [D], F32, kind="ExternalInput").ap()
    mkr = nc.dram_tensor("L0_ffn_rail_maker", [FNH, RAILS, RW, RW], F32, kind="ExternalInput").ap()
    upj = nc.dram_tensor("L0_ffn_rail_up_proj", [FNH, RAILS, RW * FMULT, RW], F32, kind="ExternalInput").ap()
    wc = nc.dram_tensor("L0_ffn_w_cproj", [D, F_TOT], F32, kind="ExternalInput").ap()
    x_out = nc.dram_tensor("out", [n_seq, T, D], F32, kind="ExternalOutput").ap()

    dbg_outs = {}
    if dbg:
        for nm, sh, dt in (("d_lnt", [128, EC * T], BF16), ("d_qt", [128, NP * T], BF16),
                           ("d_kt", [128, NP * T], BF16), ("d_vp", [128, TTILES * H * 65], BF16),
                           ("d_bias2", [128, 2 * NP], F32), ("d_ctxn", [128, NP * QCH], BF16),
                           ("d_wct", [F_TOT, D], BF16), ("d_hsup", [128, SUPC * QCH], BF16),
                           ("d_ln2t", [128, EC * T], BF16),
                           ("d_mkbd", [128, 32 * 128], BF16), ("d_upbd", [128, 64 * 128], BF16)):
            dbg_outs[nm] = nc.dram_tensor(nm, sh, dt, kind="ExternalOutput").ap()
    wct_dram = nc.dram_tensor("wct_scratch", [F_TOT, D], BF16, kind="Internal").ap()
    rails_dram = nc.dram_tensor("rails_scratch", [128, 96 * 128], BF16, kind="Internal").ap()

    ctx = ExitStack()
    kb = KB(nc, ctx)
    TE, VE, ACT, POOL, SP = nc.tensor, nc.vector, nc.scalar, nc.gpsimd, nc.sync

    _ncnt = [0]

    def uname(n):
        _ncnt[0] += 1
        return f"{n}_{_ncnt[0]}"

    def sb(name, shape, dt):
        return ctx.enter_context(nc.sbuf_tensor(name, shape, dt))

    x_sb = sb("x_sb", [128, TTILES * D], F32)
    ident = sb("ident", [128, 128], BF16)
    maskk = sb("maskk", [128, 128], BF16)
    ones_row = sb("ones_row", [1, 128], BF16)
    ones_row_f = sb("ones_row_f", [1, 128], F32)
    negeighth = sb("negeighth", [1, 128], BF16)
    ind2 = sb("ind2", [128, 2], BF16)
    eps_col = sb("eps_col", [128, 1], F32)

    kb.fence("pool", POOL.memset(ident[:], 0.0))
    kb.inc("pool", POOL.affine_select(out=ident[:], in_=ident[:], compare_op=ALU.not_equal,
                                      fill=1.0, base=0, pattern=[[-1, 128]], channel_multiplier=1))
    kb.fence("pool", POOL.memset(maskk[:], 1.0))
    kb.inc("pool", POOL.affine_select(out=maskk[:], in_=maskk[:], compare_op=ALU.is_ge,
                                      fill=0.0, base=0, pattern=[[1, 128]], channel_multiplier=-1))
    kb.inc("pool", POOL.memset(ones_row[:], 1.0))
    kb.inc("pool", POOL.memset(ones_row_f[:], 1.0))
    kb.inc("pool", POOL.memset(negeighth[:], -0.125))
    kb.inc("pool", POOL.memset(ind2[:], 0.0))
    kb.inc("pool", POOL.memset(ind2[0:64, 0:1], 1.0))
    kb.inc("pool", POOL.memset(eps_col[:], LN_EPS))
    kb.inc("pool", POOL.memset(ind2[64:128, 1:2], 1.0))
    kb.join()

    with ExitStack() as ph:
        wcstage = [ph.enter_context(nc.sbuf_tensor(f"wcstage{i}", [128, 8 * 512], BF16)) for i in range(2)]
        wcout = [ph.enter_context(nc.sbuf_tensor(f"wcout{i}", [128, 4 * 1024], BF16)) for i in range(2)]
        ps_w = [ph.enter_context(nc.psum_tensor(f"ps_w{i}", [128, 1024], BF16)) for i in range(2)]
        stage_free = [0, 0]
        out_free = [0, 0]
        ps_free = [0, 0]
        for fw in range(32):
            par = fw % 2
            kb.wait("pool", "te", stage_free[par])
            for ecb in range(8):
                kb.dma("dgp", wcstage[par][:, ecb * 512:(ecb + 1) * 512],
                       wc[ecb * 128:(ecb + 1) * 128, fw * 512:(fw + 1) * 512])
            stage_rdy = kb.cnt["dgp"]
            kb.wait("te", "dgp", stage_rdy)
            kb.wait("te", "dsp", out_free[par])
            for fb in range(4):
                pp = fb % 2
                kb.wait("te", "act", ps_free[pp])
                for ecb in range(8):
                    ins = TE.transpose(ps_w[pp][:, ecb * 128:(ecb + 1) * 128],
                                       wcstage[par][:, ecb * 512 + fb * 128: ecb * 512 + (fb + 1) * 128],
                                       ident[:])
                mark = kb.inc("te", ins)
                kb.wait("act", "te", mark)
                ps_free[pp] = kb.inc("act", ACT.copy(wcout[par][:, fb * 1024:(fb + 1) * 1024], ps_w[pp][:]))
            stage_free[par] = kb.cnt["te"]
            kb.wait("sp", "act", kb.cnt["act"])
            out_free[par] = kb.dma(
                "dsp",
                wct_dram[fw * 512:(fw + 1) * 512, :].rearrange("(fb p) e -> p fb e", p=128),
                wcout[par][:].rearrange("p (fb e) -> p fb e", fb=4))

    kb.join()

    with ExitStack() as ph:
        up_kj = ph.enter_context(nc.sbuf_tensor("up_kj", [64, 4096], BF16))
        uptall = ph.enter_context(nc.sbuf_tensor("uptall", [128, 32 * 64], BF16))
        mk_bd = ph.enter_context(nc.sbuf_tensor("mk_bd", [128, 32 * 128], BF16))
        up_bd = ph.enter_context(nc.sbuf_tensor("up_bd", [128, 64 * 128], BF16))
        ps_r = ph.enter_context(nc.psum_tensor("ps_r", [128, 512], BF16))
        POOL.memset(mk_bd[:], 0.0)
        c_zero = kb.inc("pool", POOL.memset(up_bd[:], 0.0))
        for hh in range(FNH):
            c_up = kb.dma("dgp", up_kj[:, hh * 1024:(hh + 1) * 1024].rearrange("k (r j) -> k r j", r=64),
                          upj[hh].rearrange("r k j -> k r j"))
        kb.wait("te", "dgp", c_up)
        for h in range(FNH):
            for g in range(8):
                grp = h * 8 + g
                pp = grp % 4
                ins = TE.transpose(ps_r[:, pp * 128: pp * 128 + 64],
                                   up_kj[:, (h * 64 + 8 * g) * 16:(h * 64 + 8 * g) * 16 + 128],
                                   ident[0:64, 0:64])
                mark = kb.inc("te", ins)
                if pp == 3:
                    kb.wait("act", "te", mark)
                    lo = grp - 3
                    ins = ACT.copy(
                        uptall[:, lo * 64:(grp + 1) * 64],
                        ps_r[:].rearrange("p (q e) -> p q e", q=4)[:, :, 0:64])
                    evac = kb.inc("act", ins)
                    kb.wait("te", "act", evac)
        c_upt = kb.cnt["act"]
        kb.wait("sp", "act", c_upt)
        kb.wait("sp", "pool", c_zero)
        for h in range(FNH):
            for g in range(8):
                grp = h * 8 + g
                for rr in range(8):
                    ppair = rr // 2
                    a, bb, kr = ppair // 2, ppair % 2, rr % 2
                    slot = h * 16 + g * 2 + bb
                    kb.dma("dsp",
                           up_bd[64 * a + 32 * bb + 16 * kr: 64 * a + 32 * bb + 16 * kr + 16,
                                 slot * 128 + 64 * kr: slot * 128 + 64 * kr + 64],
                           uptall[16 * rr:16 * rr + 16, grp * 64:(grp + 1) * 64])
        for h in range(FNH):
            for r in range(RAILS):
                g, rl = r // 8, r % 8
                kb.dma("dgp",
                       mk_bd[16 * rl:16 * rl + 16,
                             (h * 8 + g) * 128 + 16 * rl:(h * 8 + g) * 128 + 16 * rl + 16],
                       mkr[h, r])
        kb.wait("sp", "dsp", kb.cnt["dsp"])
        kb.wait("sp", "dgp", kb.cnt["dgp"])
        kb.dma("dsp", rails_dram[:, 0:32 * 128], mk_bd[:])
        kb.dma("dsp", rails_dram[:, 32 * 128:96 * 128], up_bd[:])
        kb.join()

    def emit_ln(lnt, sc, ps_t, ps_sqs):
        ln_bf = sc["ln_bf"]
        nm, s2, s2b, var, nm2, rstd, nmr, lv = (sc[k] for k in
                                                ("nm", "s2", "s2b", "var", "nm2", "rstd", "nmr", "lv"))
        ps_free = [0, 0]
        sq_free = 0
        lnbf_free = 0
        for ti in range(TTILES):
            xt = x_sb[:, ti * D:(ti + 1) * D]
            kb.fence("ve", VE.tensor_reduce(nm[:], xt, AX.X, ALU.add, negate=True))
            kb.fence("ve", VE.tensor_scalar(nm[:], nm[:], 1.0 / D, None, ALU.mult))
            kb.wait("act", "ve", sq_free)
            ins = ACT.activation(ps_sqs[0], xt[:, 0:512], AF.Square)
            ins = ACT.activation(ps_sqs[1], xt[:, 512:1024], AF.Square)
            m_sq = kb.inc("act", ins)
            kb.wait("ve", "act", m_sq)
            VE.tensor_reduce(s2[:], ps_sqs[0], AX.X, ALU.add)
            kb.fence("ve", VE.tensor_reduce(s2b[:], ps_sqs[1], AX.X, ALU.add))
            ins = VE.tensor_add(s2[:], s2[:], s2b[:])
            sq_free = kb.fence("ve", ins)
            VE.tensor_scalar(var[:], s2[:], 1.0 / D, None, ALU.mult)
            kb.fence("ve", VE.tensor_mul(nm2[:], nm[:], nm[:]))
            ins = VE.tensor_sub(var[:], var[:], nm2[:])
            m_v = kb.fence("ve", ins)
            kb.wait("act", "ve", m_v)
            kb.fence("act", ACT.activation(lv[:], var[:], AF.Ln, bias=eps_col[:]))
            ins = ACT.activation(rstd[:], lv[:], AF.Exp, scale=-0.5)
            m_r = kb.inc("act", ins)
            kb.wait("ve", "act", m_r)
            ins = VE.tensor_mul(nmr[:], nm[:], rstd[:])
            kb.fence("ve", ins)
            kb.wait("ve", "te", lnbf_free)
            ins = VE.tensor_scalar(ln_bf[:], xt, rstd[:], nmr[:], ALU.mult, ALU.add)
            m_ln = kb.inc("ve", ins)
            kb.wait("te", "ve", m_ln)
            for half in range(2):
                pp = half
                kb.wait("te", "act", ps_free[pp])
                for cc in range(4):
                    c = half * 4 + cc
                    ins = TE.transpose(ps_t[pp][:, cc * 128:(cc + 1) * 128],
                                       ln_bf[:, c * 128:(c + 1) * 128], ident[:])
                m_t = kb.inc("te", ins)
                kb.wait("act", "te", m_t)
                ins = ACT.copy(
                    lnt[:].rearrange("p (c t) -> p c t", c=EC)[:, half * 4:(half + 1) * 4,
                                                              ti * 128:(ti + 1) * 128],
                    ps_t[pp][:].rearrange("p (c t) -> p c t", c=4))
                ps_free[pp] = kb.inc("act", ins)
            lnbf_free = kb.cnt["te"]

    def ln_scratch(scope, tag):
        f = lambda n, sh, dt: scope.enter_context(nc.sbuf_tensor(uname(n + tag), sh, dt))
        return {"ln_bf": f("ln_bf", [128, D], BF16),
                "nm": f("nm", [128, 1], F32), "s2": f("s2", [128, 1], F32),
                "s2b": f("s2b", [128, 1], F32), "var": f("var", [128, 1], F32),
                "nm2": f("nm2", [128, 1], F32), "rstd": f("rstd", [128, 1], F32),
                "nmr": f("nmr", [128, 1], F32), "lv": f("lv", [128, 1], F32)}

    def emit_attn(s, which):
        is_b = which == "b"
        with ExitStack() as lsc:
            lsb = lambda n, sh, dt: lsc.enter_context(nc.sbuf_tensor(uname(n), sh, dt))
            qt_sb = lsb("qt_sb", [128, NP * T], BF16)
            kt_sb = lsb("kt_sb", [128, NP * T], BF16)
            vp_sb = lsb("vp_sb", [128, TTILES * H * 65], BF16)
            bias2 = lsb("bias2", [128, 2 * NP], F32)

            with ExitStack() as bsc:
                bsb = lambda n, sh, dt: bsc.enter_context(nc.sbuf_tensor(uname(n), sh, dt))
                lnt = bsb("lnt", [128, EC * T], BF16)
                sc = ln_scratch(bsc, "_at")
                wstage = bsb("wstage", [128, 3 * 1024], BF16)
                bstage = bsb("bstage", [128, 2], F32)
                bvrow = bsb("bvrow", [1, 128], BF16)
                q2s = bsb("q2s", [128, 512], BF16)
                nred = bsb("nred", [2, 10], F32)
                mhat = bsb("mhat", [2, 1], BF16)
                mrow = bsb("mrow", [1, 2], BF16)
                ps_t = [bsc.enter_context(nc.psum_tensor(uname(f"ps_t{i}"), [128, 512], BF16)) for i in range(2)]
                ps_qk = [bsc.enter_context(nc.psum_tensor(uname(f"ps_qk{i}"), [128, 512], F32)) for i in range(2)]
                ps_v = [bsc.enter_context(nc.psum_tensor(uname(f"ps_v{i}"), [128, 128], F32)) for i in range(2)]
                ps_n = bsc.enter_context(nc.psum_tensor(uname("ps_n"), [2, 512], F32))
                ps_b2 = bsc.enter_context(nc.psum_tensor(uname("ps_b2"), [128, 2], F32))

                emit_ln(lnt, sc, ps_t, [ps_qk[0][:], ps_qk[1][:]])
                kb.join()

                POOL.memset(vp_sb[:], 1.0)
                c_vones = kb.inc("pool", POOL.memset(bias2[:], 0.0))
                kb.wait("ve", "pool", c_vones)
                kb.wait("act", "pool", c_vones)

                qk_free = [0, 0]
                v_free = [0, 0]
                n_free = 0
                for c in range(NP):
                    if not is_b:
                        kb.dma("dgp", wstage[:, 0:1024].rearrange("p (ec d) -> p ec d", ec=EC),
                               wq_a[:, c * 128:(c + 1) * 128].rearrange("(ec p) d -> p ec d", p=128))
                        kb.dma("dgp", wstage[:, 1024:2048].rearrange("p (ec d) -> p ec d", ec=EC),
                               wk_a[:, c * 128:(c + 1) * 128].rearrange("(ec p) d -> p ec d", p=128))
                        cw = kb.dma("dgp", wstage[:, 2048:3072].rearrange("p (ec d) -> p ec d", ec=EC),
                                    wv_a[:, c * 128:(c + 1) * 128].rearrange("(ec p) d -> p ec d", p=128))
                        cb = 0
                    else:
                        for wi, wsrc in ((0, wq_b), (1, wk_b), (2, wv_b)):
                            for hh in range(2):
                                kb.dma("dgp",
                                       wstage[:, wi * 1024:(wi + 1) * 1024].rearrange(
                                           "p (ec x) -> p ec x", ec=EC)[:, :, hh * 64:(hh + 1) * 64],
                                       wsrc[2 * c + hh].rearrange("(ec p) d -> p ec d", p=128))
                        cw = kb.dma("dgp", bvrow[:], bv_b[2 * c:2 * c + 2].rearrange("h d -> (h d)").rearrange("(o x) -> o x", o=1))
                        kb.dma("dsp", bstage[:, 0:1], bq_b[2 * c:2 * c + 2].rearrange("h d -> (h d)").rearrange("(x o) -> x o", o=1))
                        cb = kb.dma("dsp", bstage[:, 1:2], bk_b[2 * c:2 * c + 2].rearrange("h d -> (h d)").rearrange("(x o) -> x o", o=1))
                    kb.wait("te", "dgp", cw)
                    if is_b:
                        kb.wait("ve", "dsp", cb)
                    for qk in range(2):
                        dst = qt_sb if qk == 0 else kt_sb
                        for qc in range(NQC):
                            pp = qc % 2
                            kb.wait("te", "ve" if is_b else "act", qk_free[pp])
                            for ecb in range(EC):
                                ins = TE.matmul(
                                    ps_qk[pp][:],
                                    wstage[:, qk * 1024 + ecb * 128: qk * 1024 + (ecb + 1) * 128],
                                    lnt[:, ecb * T + qc * QCH: ecb * T + (qc + 1) * QCH],
                                    start=(ecb == 0), stop=(ecb == EC - 1))
                            mm = kb.inc("te", ins)
                            if is_b:
                                kb.wait("ve", "te", mm)
                                ins = VE.tensor_scalar(dst[:, c * T + qc * QCH: c * T + (qc + 1) * QCH],
                                                       ps_qk[pp][:], bstage[:, qk:qk + 1], None, ALU.add)
                                qk_free[pp] = kb.inc("ve", ins)
                            else:
                                kb.wait("act", "te", mm)
                                ins = ACT.copy(dst[:, c * T + qc * QCH: c * T + (qc + 1) * QCH], ps_qk[pp][:])
                                qk_free[pp] = kb.inc("act", ins)
                    for ti in range(TTILES):
                        pp = ti % 2
                        kb.wait("te", "ve", v_free[pp])
                        for ecb in range(EC):
                            ins = TE.matmul(ps_v[pp][:],
                                            lnt[:, ecb * T + ti * 128: ecb * T + (ti + 1) * 128],
                                            wstage[:, 2048 + ecb * 128: 2048 + (ecb + 1) * 128],
                                            start=(ecb == 0), stop=(not is_b and ecb == EC - 1))
                        if is_b:
                            ins = TE.matmul(ps_v[pp][:], ones_row[:], bvrow[:], start=False, stop=True)
                        mm = kb.inc("te", ins)
                        kb.wait("ve", "te", mm)
                        dst = vp_sb[:, ti * H * 65 + 2 * c * 65: ti * H * 65 + 2 * c * 65 + 130]
                        ins = VE.tensor_copy(dst.rearrange("p (h x) -> p h x", h=2)[:, :, 0:64],
                                             ps_v[pp][:].rearrange("p (h x) -> p h x", h=2))
                        v_free[pp] = kb.inc("ve", ins)
                    for qk in range(2):
                        src = qt_sb if qk == 0 else kt_sb
                        for qc in range(NQC):
                            kb.wait("act", "ve", n_free)
                            ins = ACT.activation(q2s[:], src[:, c * T + qc * QCH: c * T + (qc + 1) * QCH],
                                                 AF.Square)
                            m_sq = kb.inc("act", ins)
                            kb.wait("te", "act", m_sq)
                            kb.wait("te", "ve", n_free)
                            ins = TE.matmul(ps_n[:], ind2[:], q2s[:], start=True, stop=True)
                            mm = kb.inc("te", ins)
                            kb.wait("ve", "te", mm)
                            ins = VE.tensor_reduce(nred[:, qk * 4 + qc: qk * 4 + qc + 1], ps_n[:],
                                                   AX.X, ALU.max)
                            n_free = kb.fence("ve", ins)
                    kb.fence("ve", VE.tensor_reduce(nred[:, 8:9], nred[:, 0:4], AX.X, ALU.max))
                    ins = VE.tensor_reduce(nred[:, 9:10], nred[:, 4:8], AX.X, ALU.max)
                    m_n = kb.fence("ve", ins)
                    kb.wait("act", "ve", m_n)
                    kb.fence("act", ACT.activation(nred[:, 1:2], nred[:, 8:9], AF.Ln))
                    ins = ACT.activation(nred[:, 2:3], nred[:, 9:10], AF.Ln)
                    m_l = kb.fence("act", ins)
                    kb.wait("ve", "act", m_l)
                    ins = VE.tensor_add(nred[:, 3:4], nred[:, 1:2], nred[:, 2:3])
                    m_a = kb.fence("ve", ins)
                    kb.wait("act", "ve", m_a)
                    ins = ACT.activation(mhat[:], nred[:, 3:4], AF.Exp, scale=0.5)
                    m_e = kb.inc("act", ins)
                    kb.wait("sp", "act", m_e)
                    kb.dma("dsp", mrow[0:1, 0:1], mhat[0:1, 0:1])
                    c_mr = kb.dma("dsp", mrow[0:1, 1:2], mhat[1:2, 0:1])
                    kb.wait("te", "dsp", c_mr)
                    ins = TE.matmul(ps_b2[:], negeighth[:], mrow[:], start=True, stop=True)
                    mm = kb.inc("te", ins)
                    kb.wait("act", "te", mm)
                    ins = ACT.copy(bias2[:, 2 * c:2 * c + 2], ps_b2[:])
                    kb.inc("act", ins)
                    kb.join()
                kb.join()
                if dbg and s == 0 and ((dbg == "b") == is_b):
                    kb.dma("dsp", dbg_outs["d_lnt"], lnt[:])
                    kb.dma("dsp", dbg_outs["d_qt"], qt_sb[:])
                    kb.dma("dsp", dbg_outs["d_kt"], kt_sb[:])
                    kb.dma("dsp", dbg_outs["d_vp"], vp_sb[:])
                    kb.dma("dsp", dbg_outs["d_bias2"], bias2[:])
                    kb.wait("sp", "dsp", kb.cnt["dsp"])
                    kb.join()

            with ExitStack() as ssc:
                ssb = lambda n, sh, dt: ssc.enter_context(nc.sbuf_tensor(uname(n), sh, dt))
                wo_sb = ssb("wo_sb", [128, NP * 1024], BF16)
                ctxn = ssb("ctxn", [128, NP * QCH], BF16)
                ctxn_st = ssb("ctxn_st", [64, QCH], BF16)
                p_sb = [ssb(f"p_sb{i}", [128, QCH], BF16) for i in range(2)]
                sums65 = ssb("sums65", [65, QCH], F32)
                recip = ssb("recip", [1, QCH], F32)
                fld = ssb("fld", [64, QCH], F32)
                borow = ssb("borow", [1, 1024], BF16)
                wostg = ssb("wostg", [128, 1024], BF16)
                ps_s = [ssc.enter_context(nc.psum_tensor(uname(f"ps_s{i}"), [128, QCH], F32)) for i in range(2)]
                ps_c = [ssc.enter_context(nc.psum_tensor(uname(f"ps_c{i}"), [65, QCH], F32)) for i in range(2)]
                ps_f = ssc.enter_context(nc.psum_tensor(uname("ps_f"), [64, QCH], F32))
                ps_o = ssc.enter_context(nc.psum_tensor(uname("ps_o"), [128, 1024], F32))

                if not is_b:
                    c_wo = kb.dma("dgp", wo_sb[:].rearrange("p (pq e) -> p pq e", pq=NP),
                                  wo_a.rearrange("(pq p) e -> p pq e", p=128))
                    kb.wait("te", "dgp", c_wo)
                else:
                    c_bo = kb.dma("dgp", borow[:], bo_b.rearrange("(a e) -> a e", a=1))
                    with nc.psum_tensor(uname("ps_wo"), [128, 512], BF16) as ps_wo:
                        wops_free = 0
                        wostg_free = 0
                        wo_view = wo_sb[:].rearrange("p (pq e) -> p pq e", pq=NP)
                        for i in range(EC):
                            kb.wait("pool", "te", wostg_free)
                            c_st = kb.dma("dgp", wostg[:], wo_b[i * 128:(i + 1) * 128, :])
                            kb.wait("te", "dgp", c_st)
                            for jq in range(2):
                                kb.wait("te", "act", wops_free)
                                for jj in range(4):
                                    j = jq * 4 + jj
                                    ins = TE.transpose(ps_wo[:, jj * 128:(jj + 1) * 128],
                                                       wostg[:, j * 128:(j + 1) * 128], ident[:])
                                mm = kb.inc("te", ins)
                                kb.wait("act", "te", mm)
                                ins = ACT.copy(wo_view[:, jq * 4:(jq + 1) * 4, i * 128:(i + 1) * 128],
                                               ps_wo[:].rearrange("p (q e) -> p q e", q=4))
                                wops_free = kb.inc("act", ins)
                            wostg_free = kb.cnt["te"]
                    kb.wait("te", "dgp", c_bo)

                ctx_free = [0, 0]
                s_free = [0, 0]
                p_free = [0, 0]
                psf_free = 0
                ctxn_rd = 0
                sums_rd = 0
                recip_rd = 0
                fld_rd = 0
                st_rd = 0
                for qc in range(NQC):
                    nkt = 4 * qc + 4
                    for h in range(H):
                        c, hb = h // 2, h % 2
                        cpar = h % 2
                        for ki in range(nkt):
                            spar = ki % 2
                            qlo = max(qc * QCH, ki * 128)
                            o = qlo - qc * QCH
                            kb.wait("te", "act", s_free[spar])
                            ins = TE.matmul(
                                ps_s[spar][:, o:QCH],
                                kt_sb[64 * hb:64 * hb + 64, c * T + ki * 128: c * T + (ki + 1) * 128],
                                qt_sb[64 * hb:64 * hb + 64, c * T + qlo: c * T + qc * QCH + QCH],
                                start=True, stop=True)
                            mm_s = kb.inc("te", ins)
                            kb.wait("act", "te", mm_s)
                            kb.wait("act", "te", p_free[spar])
                            ins = ACT.activation(p_sb[spar][:, o:QCH], ps_s[spar][:, o:QCH], AF.Exp,
                                                 bias=bias2[:, 2 * c + hb: 2 * c + hb + 1], scale=0.125)
                            s_free[spar] = kb.inc("act", ins)
                            mask_mark = None
                            if ki >= 4 * qc:
                                kb.wait("pool", "act", s_free[spar])
                                ins = POOL.tensor_mul(p_sb[spar][:, o:o + 128],
                                                      p_sb[spar][:, o:o + 128], maskk[:])
                                mask_mark = kb.inc("pool", ins)
                            kb.wait("te", "act", s_free[spar])
                            if mask_mark is not None:
                                kb.wait("te", "pool", mask_mark)
                            if ki == 0:
                                kb.wait("te", "ve", ctx_free[cpar])
                            ins = TE.matmul(
                                ps_c[cpar][:, o:QCH],
                                vp_sb[:, ki * H * 65 + h * 65: ki * H * 65 + (h + 1) * 65],
                                p_sb[spar][:, o:QCH],
                                start=(ki == 0), stop=(ki == nkt - 1))
                            p_free[spar] = kb.inc("te", ins)
                        mm_ctx = kb.cnt["te"]
                        kb.wait("act", "te", mm_ctx)
                        kb.wait("act", "dsp", sums_rd)
                        ins = ACT.copy(sums65[64:65, :], ps_c[cpar][64:65, :])
                        m_s = kb.inc("act", ins)
                        kb.wait("sp", "act", m_s)
                        kb.wait("sp", "te", recip_rd)
                        sums_rd = kb.dma("dsp", recip[:], sums65[64:65, :])
                        kb.wait("act", "dsp", sums_rd)
                        kb.fence("act", ACT.activation(recip[:], recip[:], AF.Ln))
                        ins = ACT.activation(recip[:], recip[:], AF.Exp, scale=-1.0)
                        m_r = kb.inc("act", ins)
                        kb.wait("te", "act", m_r)
                        kb.wait("te", "act", psf_free)
                        ins = TE.matmul(ps_f[:], ones_row_f[0:1, 0:64], recip[:], start=True, stop=True)
                        recip_rd = kb.inc("te", ins)
                        kb.wait("act", "te", recip_rd)
                        kb.wait("act", "ve", fld_rd)
                        ins = ACT.copy(fld[:], ps_f[:])
                        psf_free = kb.inc("act", ins)
                        kb.wait("ve", "act", psf_free)
                        kb.wait("ve", "te", ctxn_rd)
                        if hb == 0:
                            ins = VE.tensor_mul(ctxn[0:64, c * QCH:(c + 1) * QCH], ps_c[cpar][0:64, :], fld[:])
                            fld_rd = ctx_free[cpar] = kb.inc("ve", ins)
                        else:
                            kb.wait("ve", "dsp", st_rd)
                            ins = VE.tensor_mul(ctxn_st[:], ps_c[cpar][0:64, :], fld[:])
                            fld_rd = ctx_free[cpar] = kb.inc("ve", ins)
                            kb.wait("sp", "ve", ctx_free[cpar])
                            kb.wait("sp", "te", ctxn_rd)
                            st_rd = kb.dma("dsp", ctxn[64:128, c * QCH:(c + 1) * QCH], ctxn_st[:])
                    kb.wait("te", "ve", kb.cnt["ve"])
                    kb.wait("te", "dsp", kb.cnt["dsp"])
                    o_free = 0
                    for tt in range(4):
                        ti = qc * 4 + tt
                        kb.wait("te", "ve", o_free)
                        for pr in range(NP):
                            for eh in range(2):
                                ins = TE.matmul(
                                    ps_o[:, eh * 512:(eh + 1) * 512],
                                    ctxn[:, pr * QCH + tt * 128: pr * QCH + (tt + 1) * 128],
                                    wo_sb[:, pr * 1024 + eh * 512: pr * 1024 + (eh + 1) * 512],
                                    start=(pr == 0), stop=(pr == NP - 1 and not is_b))
                                if is_b and pr == NP - 1:
                                    ins = TE.matmul(ps_o[:, eh * 512:(eh + 1) * 512],
                                                    ones_row[:], borow[:, eh * 512:(eh + 1) * 512],
                                                    start=False, stop=True)
                        mm_o = kb.inc("te", ins)
                        kb.wait("ve", "te", mm_o)
                        xt = x_sb[:, ti * D:(ti + 1) * D]
                        ins = VE.tensor_add(xt, ps_o[:], xt)
                        o_free = kb.inc("ve", ins)
                    ctxn_rd = kb.cnt["te"]
                    if dbg and s == 0 and ((dbg == "b") == is_b) and qc == 0:
                        kb.join(("sp",))
                        kb.dma("dsp", dbg_outs["d_ctxn"], ctxn[:])
                        kb.wait("sp", "dsp", kb.cnt["dsp"])
                        kb.join()
                kb.join()

    def emit_ffn(s):
        with ExitStack() as fsc:
            fsb = lambda n, sh, dt: fsc.enter_context(nc.sbuf_tensor(uname(n), sh, dt))
            ln2t = fsb("ln2t", [128, EC * T], BF16)
            mk_bd = fsb("mk_bd2", [128, 32 * 128], BF16)
            up_bd = fsb("up_bd2", [128, 64 * 128], BF16)
            wct = [fsb(f"wct{i}", [128, SUPC * 1024], BF16) for i in range(2)]
            hsup = fsb("hsup", [128, SUPC * QCH], BF16)
            ysb = [fsb(f"ysb{i}", [128, QCH], BF16) for i in range(2)]
            kb.dma("dsp", mk_bd[:], rails_dram[:, 0:32 * 128])
            c_rails = kb.dma("dsp", up_bd[:], rails_dram[:, 32 * 128:96 * 128])
            with ExitStack() as psc:
                sc = ln_scratch(psc, "_ff")
                ps_t = [psc.enter_context(nc.psum_tensor(uname(f"ps_t2{i}"), [128, 512], BF16)) for i in range(2)]
                ps_sq = psc.enter_context(nc.psum_tensor(uname("ps_sq2"), [128, 1024], F32))
                emit_ln(ln2t, sc, ps_t, [ps_sq[:, 0:512], ps_sq[:, 512:1024]])
                kb.join()
            ps_y = [fsc.enter_context(nc.psum_tensor(uname(f"ps_y{i}"), [128, QCH], F32)) for i in range(2)]
            ps_z = [fsc.enter_context(nc.psum_tensor(uname(f"ps_z{i}"), [128, QCH], F32)) for i in range(2)]
            ps_o = fsc.enter_context(nc.psum_tensor(uname("ps_o2"), [128, 1024], F32))

            wct_rdy = [0, 0]
            wct_free = [0, 0]
            wct_rdy[0] = kb.dma("dsp", wct[0][:].rearrange("p (ch e) -> p ch e", ch=SUPC),
                                wct_dram[0:SUPF, :].rearrange("(ch p) e -> p ch e", p=128))
            if dbg:
                kb.join(("sp",))
                kb.dma("dsp", dbg_outs["d_wct"], wct_dram[:])
                kb.dma("dsp", dbg_outs["d_ln2t"], ln2t[:])
                kb.dma("dsp", dbg_outs["d_mkbd"], mk_bd[:])
                kb.dma("dsp", dbg_outs["d_upbd"], up_bd[:])
                kb.wait("sp", "dsp", kb.cnt["dsp"])
                kb.join()
            ysb_free = [0, 0]
            psy_free = [0, 0]
            z_free = [0, 0]
            o_free = 0
            kb.wait("te", "dsp", c_rails)
            for sup in range(NSUP):
                wpar = sup % 2
                if sup + 1 < NSUP:
                    npar = (sup + 1) % 2
                    kb.wait("sp", "te", wct_free[npar])
                    wct_rdy[npar] = kb.dma("dsp", wct[npar][:].rearrange("p (ch e) -> p ch e", ch=SUPC),
                                           wct_dram[(sup + 1) * SUPF:(sup + 2) * SUPF, :].rearrange(
                                               "(ch p) e -> p ch e", p=128))
                h = sup // 2
                gs = [4 * (sup % 2) + i for i in range(4)]
                kb.wait("te", "dsp", wct_rdy[wpar])
                for t5 in range(4):
                    for gi, g in enumerate(gs):
                        ypar = gi % 2
                        kb.wait("te", "ve", psy_free[ypar])
                        ins = TE.matmul(ps_y[ypar][:],
                                        mk_bd[:, (h * 8 + g) * 128:(h * 8 + g + 1) * 128],
                                        ln2t[:, g * T + t5 * QCH: g * T + (t5 + 1) * QCH],
                                        start=True, stop=True)
                        mm_y = kb.inc("te", ins)
                        kb.wait("ve", "te", mm_y)
                        kb.wait("ve", "te", ysb_free[ypar])
                        ins = VE.tensor_copy(ysb[ypar][:], ps_y[ypar][:])
                        psy_free[ypar] = kb.inc("ve", ins)
                        kb.wait("te", "ve", psy_free[ypar])
                        for ch4 in range(4):
                            a = ch4 // 2
                            slot = h * 16 + g * 2 + (ch4 % 2)
                            zpar = ch4 % 2
                            kb.wait("te", "act", z_free[zpar])
                            ins = TE.matmul(ps_z[zpar][:],
                                            up_bd[64 * a:64 * a + 64, slot * 128:(slot + 1) * 128],
                                            ysb[ypar][64 * a:64 * a + 64, :],
                                            start=True, stop=True)
                            mm_z = kb.inc("te", ins)
                            kb.wait("act", "te", mm_z)
                            chl = gi * 4 + ch4
                            ins = ACT.activation(hsup[:, chl * QCH:(chl + 1) * QCH], ps_z[zpar][:], AF.Gelu)
                            z_free[zpar] = kb.inc("act", ins)
                        ysb_free[ypar] = kb.cnt["te"]
                    m_h = kb.cnt["act"]
                    kb.wait("te", "act", m_h)
                    if dbg and sup == 0 and t5 == 0:
                        kb.join(("sp",))
                        kb.dma("dsp", dbg_outs["d_hsup"], hsup[:])
                        kb.wait("sp", "dsp", kb.cnt["dsp"])
                        kb.join()
                    for tt in range(4):
                        ti = t5 * 4 + tt
                        kb.wait("te", "ve", o_free)
                        for chl in range(SUPC):
                            for eh in range(2):
                                ins = TE.matmul(
                                    ps_o[:, eh * 512:(eh + 1) * 512],
                                    hsup[:, chl * QCH + tt * 128: chl * QCH + (tt + 1) * 128],
                                    wct[wpar][:, chl * 1024 + eh * 512: chl * 1024 + (eh + 1) * 512],
                                    start=(chl == 0), stop=(chl == SUPC - 1))
                        mm_o = kb.inc("te", ins)
                        kb.wait("ve", "te", mm_o)
                        xt = x_sb[:, ti * D:(ti + 1) * D]
                        ins = VE.tensor_add(xt, ps_o[:], xt)
                        o_free = kb.inc("ve", ins)
                wct_free[wpar] = kb.cnt["te"]
            kb.join()

    for s in range(n_seq):
        c_x = kb.dma("dsp", x_sb[:].rearrange("p (ti e) -> p ti e", ti=TTILES),
                      x_in[s].rearrange("(ti p) e -> p ti e", p=128))
        for e in ("te", "ve", "act", "pool"):
            kb.wait(e, "dsp", c_x)
        if "a" in stages:
            emit_attn(s, "a")
        if "f1" in stages:
            emit_ffn(s)
        if "b" in stages:
            emit_attn(s, "b")
        if "f2" in stages:
            emit_ffn(s)
        kb.join(("sp",))
        kb.dma("dsp", x_out[s].rearrange("(ti p) e -> p ti e", p=128),
               x_sb[:].rearrange("p (ti e) -> p ti e", ti=TTILES))
    kb.wait("sp", "dsp", kb.cnt["dsp"])
    ctx.close()
    return nc


def kernel(**inputs):
    x = inputs["L0_attn_a_x"]
    nc = build_program()
    in_maps = []
    for c in range(NCORES):
        m = {"L0_attn_a_x": np.ascontiguousarray(x[SPC * c:SPC * (c + 1)])}
        for k, v in inputs.items():
            if k != "L0_attn_a_x":
                m[k] = v
        in_maps.append(m)
    res = run_bass_kernel_spmd(nc, in_maps, core_ids=list(range(NCORES)))
    return np.concatenate([res.results[c]["out"] for c in range(NCORES)], axis=0)


# version 11
# speedup vs baseline: 1.4237x; 1.4237x over previous
import numpy as np
from contextlib import ExitStack

import concourse.bass as bass
import concourse.mybir as mybir
from concourse.bass_utils import run_bass_kernel_spmd

F32 = mybir.dt.float32
BF16 = mybir.dt.bfloat16
AF = mybir.ActivationFunctionType
ALU = mybir.AluOpType
AX = mybir.AxisListType

B, T, D = 16, 2048, 1024
NCORES = 8
SPC = B // NCORES
H, DK = 16, 64
NP = H // 2
EC = D // 128
TTILES = T // 128
QCH = 512
NQC = T // QCH
FNH, RAILS, RW, FMULT = 4, 64, 16, 4
F_TOT = FNH * RAILS * RW * FMULT
SUPF = 2048
NSUP = F_TOT // SUPF
SUPC = SUPF // 128
LN_EPS = 1e-5
DEFER_N2 = True
PIPE_KI = True
MASK_ON_POOL = False
PAIR_JOIN = False

ALL_SEMS = ("te", "ve", "act", "pool", "dsp", "dgp", "dsc")


class KB:
    def __init__(self, nc, ctx):
        self.nc = nc
        self.sems, self.cnt = {}, {}
        for name in ALL_SEMS:
            self.sems[name] = ctx.enter_context(nc.semaphore("s_" + name))
            self.cnt[name] = 0
        self.engs = {"te": nc.tensor, "ve": nc.vector, "act": nc.scalar,
                     "pool": nc.gpsimd, "sp": nc.sync}

    def inc(self, name, instr, k=1):
        instr.then_inc(self.sems[name], k)
        self.cnt[name] += k
        return self.cnt[name]

    def dma(self, qname, out, in_, **kw):
        eng = {"dsp": self.nc.sync, "dgp": self.nc.gpsimd, "dsc": self.nc.scalar}[qname]
        return self.inc(qname, eng.dma_start(out=out, in_=in_, **kw), 16)

    def wait(self, eng_name, sem_name, val=None):
        v = self.cnt[sem_name] if val is None else val
        if v > 0:
            self.engs[eng_name].wait_ge(self.sems[sem_name], v)

    def join(self, engines=("te", "ve", "act", "pool", "sp")):
        for e in engines:
            for s in ALL_SEMS:
                self.wait(e, s)

    def fence(self, name, instr):
        m = self.inc(name, instr)
        self.engs[name].wait_ge(self.sems[name], m)
        return m

    def snap(self):
        return dict(self.cnt)


def build_program(stages=("a", "f1", "b", "f2"), n_seq=SPC, dbg=False):
    if dbg is True:
        dbg = "a"
    nc = bass.Bass("TRN2", target_bir_lowering=False, debug=False)

    x_in = nc.dram_tensor("L0_attn_a_x", [n_seq, T, D], F32, kind="ExternalInput").ap()
    wq_a = nc.dram_tensor("L0_attn_a_Wq", [D, D], F32, kind="ExternalInput").ap()
    wk_a = nc.dram_tensor("L0_attn_a_Wk", [D, D], F32, kind="ExternalInput").ap()
    wv_a = nc.dram_tensor("L0_attn_a_Wv", [D, D], F32, kind="ExternalInput").ap()
    wo_a = nc.dram_tensor("L0_attn_a_Wo", [D, D], F32, kind="ExternalInput").ap()
    wq_b = nc.dram_tensor("L0_attn_b_Wq", [H, D, DK], F32, kind="ExternalInput").ap()
    bq_b = nc.dram_tensor("L0_attn_b_bq", [H, DK], F32, kind="ExternalInput").ap()
    wk_b = nc.dram_tensor("L0_attn_b_Wk", [H, D, DK], F32, kind="ExternalInput").ap()
    bk_b = nc.dram_tensor("L0_attn_b_bk", [H, DK], F32, kind="ExternalInput").ap()
    wv_b = nc.dram_tensor("L0_attn_b_Wv", [H, D, DK], F32, kind="ExternalInput").ap()
    bv_b = nc.dram_tensor("L0_attn_b_bv", [H, DK], F32, kind="ExternalInput").ap()
    wo_b = nc.dram_tensor("L0_attn_b_Wo", [D, D], F32, kind="ExternalInput").ap()
    bo_b = nc.dram_tensor("L0_attn_b_bo", [D], F32, kind="ExternalInput").ap()
    mkr = nc.dram_tensor("L0_ffn_rail_maker", [FNH, RAILS, RW, RW], F32, kind="ExternalInput").ap()
    upj = nc.dram_tensor("L0_ffn_rail_up_proj", [FNH, RAILS, RW * FMULT, RW], F32, kind="ExternalInput").ap()
    wc = nc.dram_tensor("L0_ffn_w_cproj", [D, F_TOT], F32, kind="ExternalInput").ap()
    x_out = nc.dram_tensor("out", [n_seq, T, D], F32, kind="ExternalOutput").ap()

    dbg_outs = {}
    if dbg:
        for nm, sh, dt in (("d_lnt", [128, EC * T], BF16), ("d_qt", [128, NP * T], BF16),
                           ("d_kt", [128, NP * T], BF16), ("d_vp", [128, TTILES * H * 65], BF16),
                           ("d_bias2", [128, 2 * NP], F32), ("d_ctxn", [128, NP * QCH], BF16),
                           ("d_wct", [F_TOT, D], BF16), ("d_hsup", [128, SUPC * QCH], BF16),
                           ("d_ln2t", [128, EC * T], BF16),
                           ("d_mkbd", [128, 32 * 128], BF16), ("d_upbd", [128, 64 * 128], BF16),
                           ("d_p15", [128, 128], BF16), ("d_rec", [1, QCH], F32),
                           ("d_s15", [128, 128], F32)):
            dbg_outs[nm] = nc.dram_tensor(nm, sh, dt, kind="ExternalOutput").ap()
    wct_dram = nc.dram_tensor("wct_scratch", [F_TOT, D], BF16, kind="Internal").ap()
    rails_dram = nc.dram_tensor("rails_scratch", [128, 96 * 128], BF16, kind="Internal").ap()

    ctx = ExitStack()
    kb = KB(nc, ctx)
    TE, VE, ACT, POOL, SP = nc.tensor, nc.vector, nc.scalar, nc.gpsimd, nc.sync

    _ncnt = [0]

    def uname(n):
        _ncnt[0] += 1
        return f"{n}_{_ncnt[0]}"

    def sb(name, shape, dt):
        return ctx.enter_context(nc.sbuf_tensor(name, shape, dt))

    x_sb = sb("x_sb", [128, TTILES * D], F32)
    ident = sb("ident", [128, 128], BF16)
    maskk = sb("maskk", [128, 128], BF16)
    ones_row = sb("ones_row", [1, 128], BF16)
    ones_row_f = sb("ones_row_f", [1, 128], F32)
    negeighth = sb("negeighth", [1, 128], BF16)
    ind2 = sb("ind2", [128, 2], BF16)
    eps_col = sb("eps_col", [128, 1], F32)

    kb.fence("pool", POOL.memset(ident[:], 0.0))
    kb.fence("pool", POOL.affine_select(out=ident[:], in_=ident[:], compare_op=ALU.not_equal,
                                        fill=1.0, base=0, pattern=[[-1, 128]], channel_multiplier=1))
    kb.fence("pool", POOL.memset(maskk[:], 1.0))
    kb.fence("pool", POOL.affine_select(out=maskk[:], in_=maskk[:], compare_op=ALU.is_ge,
                                        fill=0.0, base=0, pattern=[[1, 128]], channel_multiplier=-1))
    kb.fence("pool", POOL.memset(ones_row[:], 1.0))
    kb.fence("pool", POOL.memset(ones_row_f[:], 1.0))
    kb.fence("pool", POOL.memset(negeighth[:], -0.125))
    kb.fence("pool", POOL.memset(ind2[:], 0.0))
    kb.fence("pool", POOL.memset(ind2[0:64, 0:1], 1.0))
    kb.fence("pool", POOL.memset(eps_col[:], LN_EPS))
    kb.fence("pool", POOL.memset(ind2[64:128, 1:2], 1.0))
    kb.join()

    with ExitStack() as ph:
        wcstage = [ph.enter_context(nc.sbuf_tensor(f"wcstage{i}", [128, 8 * 512], BF16)) for i in range(2)]
        wcout = [ph.enter_context(nc.sbuf_tensor(f"wcout{i}", [128, 4 * 1024], BF16)) for i in range(2)]
        ps_w = [ph.enter_context(nc.psum_tensor(f"ps_w{i}", [128, 1024], BF16)) for i in range(2)]
        stage_free = [0, 0]
        out_free = [0, 0]
        ps_free = [0, 0]
        for fw in range(32):
            par = fw % 2
            kb.wait("pool", "te", stage_free[par])
            kb.wait("pool", "dgp", kb.cnt["dgp"])
            for ecb in range(8):
                kb.dma("dgp", wcstage[par][:, ecb * 512:(ecb + 1) * 512],
                       wc[ecb * 128:(ecb + 1) * 128, fw * 512:(fw + 1) * 512])
            stage_rdy = kb.cnt["dgp"]
            kb.wait("te", "dgp", stage_rdy)
            kb.wait("te", "dsp", out_free[par])
            for fb in range(4):
                pp = fb % 2
                kb.wait("te", "act", ps_free[pp])
                for ecb in range(8):
                    ins = TE.transpose(ps_w[pp][:, ecb * 128:(ecb + 1) * 128],
                                       wcstage[par][:, ecb * 512 + fb * 128: ecb * 512 + (fb + 1) * 128],
                                       ident[:])
                mark = kb.inc("te", ins)
                kb.wait("act", "te", mark)
                ps_free[pp] = kb.inc("act", ACT.copy(wcout[par][:, fb * 1024:(fb + 1) * 1024], ps_w[pp][:]))
            stage_free[par] = kb.cnt["te"]
            kb.wait("sp", "act", kb.cnt["act"])
            kb.wait("sp", "dsp", kb.cnt["dsp"])
            out_free[par] = kb.dma(
                "dsp",
                wct_dram[fw * 512:(fw + 1) * 512, :].rearrange("(fb p) e -> p fb e", p=128),
                wcout[par][:].rearrange("p (fb e) -> p fb e", fb=4))

    kb.join()

    with ExitStack() as ph:
        up_kj = ph.enter_context(nc.sbuf_tensor("up_kj", [64, 4096], BF16))
        uptall = ph.enter_context(nc.sbuf_tensor("uptall", [128, 32 * 64], BF16))
        mk_bd = ph.enter_context(nc.sbuf_tensor("mk_bd", [128, 32 * 128], BF16))
        up_bd = ph.enter_context(nc.sbuf_tensor("up_bd", [128, 64 * 128], BF16))
        ps_r = ph.enter_context(nc.psum_tensor("ps_r", [128, 512], BF16))
        kb.fence("pool", POOL.memset(mk_bd[:], 0.0))
        c_zero = kb.fence("pool", POOL.memset(up_bd[:], 0.0))
        for hh in range(FNH):
            c_up = kb.dma("dgp", up_kj[:, hh * 1024:(hh + 1) * 1024].rearrange("k (r j) -> k r j", r=64),
                          upj[hh].rearrange("r k j -> k r j"))
        kb.wait("te", "dgp", c_up)
        for h in range(FNH):
            for g in range(8):
                grp = h * 8 + g
                pp = grp % 4
                ins = TE.transpose(ps_r[:, pp * 128: pp * 128 + 64],
                                   up_kj[:, (h * 64 + 8 * g) * 16:(h * 64 + 8 * g) * 16 + 128],
                                   ident[0:64, 0:64])
                mark = kb.inc("te", ins)
                if pp == 3:
                    kb.wait("act", "te", mark)
                    lo = grp - 3
                    ins = ACT.copy(
                        uptall[:, lo * 64:(grp + 1) * 64],
                        ps_r[:].rearrange("p (q e) -> p q e", q=4)[:, :, 0:64])
                    evac = kb.inc("act", ins)
                    kb.wait("te", "act", evac)
        c_upt = kb.cnt["act"]
        kb.wait("sp", "act", c_upt)
        kb.wait("sp", "pool", c_zero)
        for h in range(FNH):
            for g in range(8):
                grp = h * 8 + g
                for rr in range(8):
                    ppair = rr // 2
                    a, bb, kr = ppair // 2, ppair % 2, rr % 2
                    slot = h * 16 + g * 2 + bb
                    kb.dma("dsp",
                           up_bd[64 * a + 32 * bb + 16 * kr: 64 * a + 32 * bb + 16 * kr + 16,
                                 slot * 128 + 64 * kr: slot * 128 + 64 * kr + 64],
                           uptall[16 * rr:16 * rr + 16, grp * 64:(grp + 1) * 64])
        for h in range(FNH):
            for r in range(RAILS):
                g, rl = r // 8, r % 8
                kb.dma("dgp",
                       mk_bd[16 * rl:16 * rl + 16,
                             (h * 8 + g) * 128 + 16 * rl:(h * 8 + g) * 128 + 16 * rl + 16],
                       mkr[h, r])
        kb.wait("sp", "dsp", kb.cnt["dsp"])
        kb.wait("sp", "dgp", kb.cnt["dgp"])
        kb.dma("dsp", rails_dram[:, 0:32 * 128], mk_bd[:])
        kb.dma("dsp", rails_dram[:, 32 * 128:96 * 128], up_bd[:])
        kb.join()

    def emit_ln(lnt, sc, ps_t, ps_sqs):
        ln_bf = sc["ln_bf"]
        st = sc["st"]
        NT = TTILES
        nm_a = st[:, 0 * NT:1 * NT]
        s2a = st[:, 1 * NT:2 * NT]
        s2b = st[:, 2 * NT:3 * NT]
        var_a = st[:, 3 * NT:4 * NT]
        rstd_a = st[:, 4 * NT:5 * NT]
        nmr_a = st[:, 5 * NT:6 * NT]
        sq_free = [0, 0]
        for ti in range(TTILES):
            xt = x_sb[:, ti * D:(ti + 1) * D]
            VE.tensor_reduce(nm_a[:, ti:ti + 1], xt, AX.X, ALU.add, negate=True)
            pp = ti % 2
            kb.wait("act", "ve", sq_free[pp])
            ins = ACT.activation(ps_sqs[2 * pp + 0], xt[:, 0:512], AF.Square)
            ins = ACT.activation(ps_sqs[2 * pp + 1], xt[:, 512:1024], AF.Square)
            m_sq = kb.inc("act", ins)
            kb.wait("ve", "act", m_sq)
            VE.tensor_reduce(s2a[:, ti:ti + 1], ps_sqs[2 * pp + 0], AX.X, ALU.add)
            ins = VE.tensor_reduce(s2b[:, ti:ti + 1], ps_sqs[2 * pp + 1], AX.X, ALU.add)
            sq_free[pp] = kb.inc("ve", ins)
        kb.wait("ve", "ve", kb.cnt["ve"])
        kb.fence("ve", VE.tensor_add(s2a, s2a, s2b))
        kb.fence("ve", VE.tensor_scalar(var_a, s2a, 1.0 / D, None, ALU.mult))
        kb.fence("ve", VE.tensor_scalar(nm_a, nm_a, 1.0 / D, None, ALU.mult))
        kb.fence("ve", VE.tensor_mul(s2b, nm_a, nm_a))
        m_v = kb.fence("ve", VE.tensor_sub(var_a, var_a, s2b))
        kb.wait("act", "ve", m_v)
        kb.fence("act", ACT.activation(s2a, var_a, AF.Ln, bias=eps_col[:]))
        m_r = kb.fence("act", ACT.activation(rstd_a, s2a, AF.Exp, scale=-0.5))
        kb.wait("ve", "act", m_r)
        m_c = kb.fence("ve", VE.tensor_mul(nmr_a, nm_a, rstd_a))
        ps_free = [0, 0]
        lnbf_free = [0, 0]
        for ti in range(TTILES):
            xt = x_sb[:, ti * D:(ti + 1) * D]
            bp = ti % 2
            kb.wait("ve", "te", lnbf_free[bp])
            ins = VE.tensor_scalar(ln_bf[:, bp * D:(bp + 1) * D], xt,
                                   rstd_a[:, ti:ti + 1], nmr_a[:, ti:ti + 1],
                                   ALU.mult, ALU.add)
            m_ln = kb.inc("ve", ins)
            kb.wait("te", "ve", m_ln)
            for half in range(2):
                pp = half
                kb.wait("te", "act", ps_free[pp])
                for cc in range(4):
                    c = half * 4 + cc
                    ins = TE.transpose(ps_t[pp][:, cc * 128:(cc + 1) * 128],
                                       ln_bf[:, bp * D + c * 128: bp * D + (c + 1) * 128], ident[:])
                m_t = kb.inc("te", ins)
                kb.wait("act", "te", m_t)
                ins = ACT.copy(
                    lnt[:].rearrange("p (c t) -> p c t", c=EC)[:, half * 4:(half + 1) * 4,
                                                              ti * 128:(ti + 1) * 128],
                    ps_t[pp][:].rearrange("p (c t) -> p c t", c=4))
                ps_free[pp] = kb.inc("act", ins)
            lnbf_free[bp] = kb.cnt["te"]

    def ln_scratch(scope, tag):
        f = lambda n, sh, dt: scope.enter_context(nc.sbuf_tensor(uname(n + tag), sh, dt))
        return {"ln_bf": f("ln_bf", [128, 2 * D], BF16),
                "st": f("st", [128, 6 * TTILES], F32)}

    def emit_attn(s, which):
        is_b = which == "b"
        with ExitStack() as lsc:
            lsb = lambda n, sh, dt: lsc.enter_context(nc.sbuf_tensor(uname(n), sh, dt))
            qt_sb = lsb("qt_sb", [128, NP * T], BF16)
            kt_sb = lsb("kt_sb", [128, NP * T], BF16)
            vp_sb = lsb("vp_sb", [128, TTILES * H * 65], BF16)
            bias2 = lsb("bias2", [128, 2 * NP], F32)
            mrow = lsb("mrow", [1, 2 * NP], BF16)

            with ExitStack() as bsc:
                bsb = lambda n, sh, dt: bsc.enter_context(nc.sbuf_tensor(uname(n), sh, dt))
                lnt = bsb("lnt", [128, EC * T], BF16)
                ps_t = [bsc.enter_context(nc.psum_tensor(uname(f"ps_t{i}"), [128, 512], BF16)) for i in range(2)]
                ps_qk = [bsc.enter_context(nc.psum_tensor(uname(f"ps_qk{i}"), [128, 512], F32)) for i in range(2)]
                ps_v = [bsc.enter_context(nc.psum_tensor(uname(f"ps_v{i}"), [128, 512], F32)) for i in range(2)]
                ps_n = [bsc.enter_context(nc.psum_tensor(uname(f"ps_n{i}"), [2, 512], F32)) for i in range(2)]
                with ExitStack() as lns:
                    sc = ln_scratch(lns, "_at")
                    emit_ln(lnt, sc, ps_t,
                            [ps_qk[0][:], ps_qk[1][:], ps_v[0][:], ps_v[1][:]])
                c_ln = kb.cnt["act"]
                c_lnv = kb.cnt["ve"]
                c_lnte = kb.cnt["te"]
                wstage = bsb("wstage", [128, 3 * 1024], BF16)
                bstage = bsb("bstage", [128, 4], F32)
                bvrow = bsb("bvrow", [1, 256], BF16)
                q2s = bsb("q2s", [128, 2 * 512], BF16)
                nred = bsb("nred", [2, 8], F32)
                nq_all = bsb("nq_all", [2, 2 * NP], F32)
                mh_all = bsb("mh_all", [2, NP], BF16)

                kb.wait("pool", "ve", c_lnv)
                kb.wait("pool", "te", c_lnte)
                kb.wait("sp", "ve", c_lnv)
                kb.wait("sp", "te", c_lnte)
                kb.fence("pool", POOL.memset(vp_sb[:], 1.0))
                c_vones = kb.fence("pool", POOL.memset(bias2[:], 0.0))
                kb.wait("ve", "pool", c_vones)
                kb.wait("te", "act", c_ln)

                def load_pair_w(c):
                    par = c % 2
                    if not is_b:
                        kb.dma("dgp", wstage[:, 0:1024].rearrange("p (ec d) -> p ec d", ec=EC),
                               wq_a[:, c * 128:(c + 1) * 128].rearrange("(ec p) d -> p ec d", p=128))
                        kb.dma("dgp", wstage[:, 1024:2048].rearrange("p (ec d) -> p ec d", ec=EC),
                               wk_a[:, c * 128:(c + 1) * 128].rearrange("(ec p) d -> p ec d", p=128))
                        cw = kb.dma("dgp", wstage[:, 2048:3072].rearrange("p (ec d) -> p ec d", ec=EC),
                                    wv_a[:, c * 128:(c + 1) * 128].rearrange("(ec p) d -> p ec d", p=128))
                        cb = 0
                    else:
                        for wi, wsrc in ((0, wq_b), (1, wk_b), (2, wv_b)):
                            for hh in range(2):
                                kb.dma("dgp",
                                       wstage[:, wi * 1024:(wi + 1) * 1024].rearrange(
                                           "p (ec x) -> p ec x", ec=EC)[:, :, hh * 64:(hh + 1) * 64],
                                       wsrc[2 * c + hh].rearrange("(ec p) d -> p ec d", p=128))
                        cw = kb.dma("dgp", bvrow[0:1, par * 128:(par + 1) * 128],
                                    bv_b[2 * c:2 * c + 2].rearrange("h d -> (h d)").rearrange("(o x) -> o x", o=1))
                        kb.dma("dsp", bstage[:, 2 * par:2 * par + 1],
                               bq_b[2 * c:2 * c + 2].rearrange("h d -> (h d)").rearrange("(x o) -> x o", o=1))
                        cb = kb.dma("dsp", bstage[:, 2 * par + 1:2 * par + 2],
                                    bk_b[2 * c:2 * c + 2].rearrange("h d -> (h d)").rearrange("(x o) -> x o", o=1))
                    return cw, cb

                cw, cb = load_pair_w(0)
                qk_free = [0, 0]
                v_free = [0, 0]
                n_free = [0, 0]
                q2_free = [0, 0]
                for c in range(NP):
                    par = c % 2
                    kb.wait("te", "dgp", cw)
                    if is_b:
                        kb.wait("ve", "dsp", cb)
                    for qk in range(2):
                        dst = qt_sb if qk == 0 else kt_sb
                        for qc in range(NQC):
                            pp = qc % 2
                            kb.wait("te", "ve" if is_b else "act", qk_free[pp])
                            for ecb in range(EC):
                                ins = TE.matmul(
                                    ps_qk[pp][:],
                                    wstage[:, qk * 1024 + ecb * 128: qk * 1024 + (ecb + 1) * 128],
                                    lnt[:, ecb * T + qc * QCH: ecb * T + (qc + 1) * QCH],
                                    start=(ecb == 0), stop=(ecb == EC - 1))
                            mm = kb.inc("te", ins)
                            if is_b:
                                kb.wait("ve", "te", mm)
                                ins = VE.tensor_scalar(dst[:, c * T + qc * QCH: c * T + (qc + 1) * QCH],
                                                       ps_qk[pp][:], bstage[:, 2 * par + qk:2 * par + qk + 1],
                                                       None, ALU.add)
                                qk_free[pp] = kb.inc("ve", ins)
                            else:
                                kb.wait("act", "te", mm)
                                ins = ACT.copy(dst[:, c * T + qc * QCH: c * T + (qc + 1) * QCH], ps_qk[pp][:])
                                qk_free[pp] = kb.inc("act", ins)
                    for ti in range(TTILES):
                        pp = ti % 2
                        kb.wait("te", "ve", v_free[pp])
                        for ecb in range(EC):
                            ins = TE.matmul(ps_v[pp][:, 0:128],
                                            lnt[:, ecb * T + ti * 128: ecb * T + (ti + 1) * 128],
                                            wstage[:, 2048 + ecb * 128: 2048 + (ecb + 1) * 128],
                                            start=(ecb == 0), stop=(not is_b and ecb == EC - 1))
                        if is_b:
                            ins = TE.matmul(ps_v[pp][:, 0:128], ones_row[:],
                                            bvrow[0:1, par * 128:(par + 1) * 128], start=False, stop=True)
                        mm = kb.inc("te", ins)
                        kb.wait("ve", "te", mm)
                        dstv = vp_sb[:, ti * H * 65 + 2 * c * 65: ti * H * 65 + 2 * c * 65 + 130]
                        ins = VE.tensor_copy(dstv.rearrange("p (h x) -> p h x", h=2)[:, :, 0:64],
                                             ps_v[pp][:, 0:128].rearrange("p (h x) -> p h x", h=2))
                        v_free[pp] = kb.inc("ve", ins)
                    te_pair_done = kb.cnt["te"]
                    if c + 1 < NP:
                        kb.wait("pool", "te", te_pair_done)
                        kb.wait("pool", "dgp", kb.cnt["dgp"])
                        if is_b:
                            kb.wait("sp", "ve", qk_free[0])
                            kb.wait("sp", "ve", qk_free[1])
                        cw, cb = load_pair_w(c + 1)
                    for qk in range(2):
                        src = qt_sb if qk == 0 else kt_sb
                        for qc in range(NQC):
                            pp = qc % 2
                            kb.wait("act", "te", q2_free[pp])
                            kb.wait("act", "ve" if is_b else "act", qk_free[0])
                            kb.wait("act", "ve" if is_b else "act", qk_free[1])
                            ins = ACT.activation(q2s[:, pp * 512:(pp + 1) * 512],
                                                 src[:, c * T + qc * QCH: c * T + (qc + 1) * QCH], AF.Square)
                            m_sq = kb.inc("act", ins)
                            kb.wait("te", "act", m_sq)
                            kb.wait("te", "ve", n_free[pp])
                            ins = TE.matmul(ps_n[pp][:], ind2[:], q2s[:, pp * 512:(pp + 1) * 512],
                                            start=True, stop=True)
                            q2_free[pp] = mm = kb.inc("te", ins)
                            kb.wait("ve", "te", mm)
                            ins = VE.tensor_reduce(nred[:, qk * 4 + qc: qk * 4 + qc + 1], ps_n[pp][:],
                                                   AX.X, ALU.max)
                            n_free[pp] = kb.inc("ve", ins)
                    kb.wait("ve", "ve", kb.cnt["ve"])
                    kb.fence("ve", VE.tensor_reduce(nq_all[:, c:c + 1], nred[:, 0:4], AX.X, ALU.max))
                    kb.fence("ve", VE.tensor_reduce(nq_all[:, NP + c:NP + c + 1], nred[:, 4:8], AX.X, ALU.max))
                    if PAIR_JOIN:
                        kb.join()
                m_nq = kb.cnt["ve"]
                kb.wait("act", "ve", m_nq)
                kb.fence("act", ACT.activation(nq_all[:, 0:NP], nq_all[:, 0:NP], AF.Ln))
                m_l = kb.fence("act", ACT.activation(nq_all[:, NP:2 * NP], nq_all[:, NP:2 * NP], AF.Ln))
                kb.wait("ve", "act", m_l)
                m_a = kb.fence("ve", VE.tensor_add(nq_all[:, 0:NP], nq_all[:, 0:NP], nq_all[:, NP:2 * NP]))
                kb.wait("act", "ve", m_a)
                m_e = kb.inc("act", ACT.activation(mh_all[:], nq_all[:, 0:NP], AF.Exp, scale=0.5))
                kb.wait("sp", "act", m_e)
                kb.dma("dsp", mrow[:].rearrange("o (p r) -> o p r", r=2)[:, :, 0:1], mh_all[0:1, :],
                       allow_slow_non_contiguous=True)
                c_mr = kb.dma("dsp", mrow[:].rearrange("o (p r) -> o p r", r=2)[:, :, 1:2], mh_all[1:2, :],
                              allow_slow_non_contiguous=True)
                kb.join()
                if dbg and s == 0 and ((dbg == "b") == is_b):
                    kb.dma("dsp", dbg_outs["d_lnt"], lnt[:])
                    kb.dma("dsp", dbg_outs["d_qt"], qt_sb[:])
                    kb.dma("dsp", dbg_outs["d_kt"], kt_sb[:])
                    kb.wait("sp", "dsp", kb.cnt["dsp"])
                    kb.join()

            with ExitStack() as ssc:
                ssb = lambda n, sh, dt: ssc.enter_context(nc.sbuf_tensor(uname(n), sh, dt))
                wo_sb = ssb("wo_sb", [128, NP * 1024], BF16)
                ctxn = ssb("ctxn", [128, NP * QCH], BF16)
                ctxn_st = ssb("ctxn_st", [64, 2 * QCH], BF16)
                p_sb = [ssb(f"p_sb{i}", [128, 2 * QCH], BF16) for i in range(2)]
                sums65 = [ssb(f"sums65{i}", [65, QCH], F32) for i in range(2)]
                recip = [ssb(f"recip{i}", [1, QCH], F32) for i in range(2)]
                ones65 = ssb("ones65", [65, 128], F32)
                fld = ssb("fld", [64, QCH], F32)
                borow = ssb("borow", [1, 1024], BF16) if is_b else None
                wostg = ssb("wostg", [128, 1024], BF16) if is_b else None
                c_o65 = kb.fence("pool", POOL.memset(ones65[64:65, :], 1.0))
                kb.wait("te", "pool", c_o65)
                with nc.psum_tensor(uname("ps_b2"), [128, 2 * NP], F32) as ps_b2:
                    kb.wait("te", "dsp", kb.cnt["dsp"])
                    mmb = kb.inc("te", TE.matmul(ps_b2[:], negeighth[:], mrow[:], start=True, stop=True))
                    kb.wait("act", "te", mmb)
                    c_b2 = kb.inc("act", ACT.copy(bias2[:], ps_b2[:]))
                    kb.wait("act", "act", c_b2)
                if not is_b:
                    c_wo = kb.dma("dgp", wo_sb[:].rearrange("p (pq e) -> p pq e", pq=NP),
                                  wo_a.rearrange("(pq p) e -> p pq e", p=128))
                else:
                    c_bo = kb.dma("dgp", borow[:], bo_b.rearrange("(a e) -> a e", a=1))
                    with nc.psum_tensor(uname("ps_wo"), [128, 512], BF16) as ps_wo:
                        wops_free = 0
                        wostg_free = 0
                        wo_view = wo_sb[:].rearrange("p (pq e) -> p pq e", pq=NP)
                        for i in range(EC):
                            kb.wait("pool", "te", wostg_free)
                            kb.wait("pool", "dgp", kb.cnt["dgp"])
                            c_st = kb.dma("dgp", wostg[:], wo_b[i * 128:(i + 1) * 128, :])
                            kb.wait("te", "dgp", c_st)
                            for jq in range(2):
                                kb.wait("te", "act", wops_free)
                                for jj in range(4):
                                    j = jq * 4 + jj
                                    ins = TE.transpose(ps_wo[:, jj * 128:(jj + 1) * 128],
                                                       wostg[:, j * 128:(j + 1) * 128], ident[:])
                                mm = kb.inc("te", ins)
                                kb.wait("act", "te", mm)
                                ins = ACT.copy(wo_view[:, jq * 4:(jq + 1) * 4, i * 128:(i + 1) * 128],
                                               ps_wo[:].rearrange("p (q e) -> p q e", q=4))
                                wops_free = kb.inc("act", ins)
                            wostg_free = kb.cnt["te"]
                    c_wo = kb.dma("dgp", borow[:], bo_b.rearrange("(a e) -> a e", a=1))

                if dbg:
                    dbg_p15 = ssb("dbg_p15", [128, 128], BF16)
                    dbg_rec = ssb("dbg_rec", [1, QCH], F32)
                    dbg_s15 = ssb("dbg_s15", [128, 128], F32)
                ps_s = [ssc.enter_context(nc.psum_tensor(uname(f"ps_s{i}"), [128, 2 * QCH], F32)) for i in range(2)]
                ps_c = [ssc.enter_context(nc.psum_tensor(uname(f"ps_c{i}"), [65, QCH], F32)) for i in range(2)]
                ps_o = ssc.enter_context(nc.psum_tensor(uname("ps_o"), [128, 1024], F32))
                ps_f = ps_o[0:64, 0:QCH]
                kb.wait("te", "dgp", c_wo)
                kb.wait("te", "act", kb.cnt["act"])
                kb.wait("te", "ve", kb.cnt["ve"])

                ctx_free = [0, 0]
                sums_free = [0, 0]
                s_free = [0, 0]
                p_free = [0, 0]
                psf_free = 0
                fld_free = 0
                ctxn_rd = 0
                st_rd = [0, 0]
                sums_dma = [0, 0]
                o_free = 0
                pend = None

                def emit_ctx_group(qc, h, grp, gpar, sfree_val, mask_mark):
                    cpar2 = h % 2
                    nkt2 = 4 * qc + 4
                    kb.wait("te", "act", sfree_val)
                    if mask_mark is not None:
                        kb.wait("te", mask_mark[0], mask_mark[1])
                    for half, ki in enumerate(grp):
                        qlo2 = max(qc * QCH, ki * 128)
                        o2 = qlo2 - qc * QCH
                        if ki == 0:
                            kb.wait("te", "ve", ctx_free[cpar2])
                        ins = TE.matmul(
                            ps_c[cpar2][:, o2:QCH],
                            vp_sb[:, ki * H * 65 + h * 65: ki * H * 65 + (h + 1) * 65],
                            p_sb[gpar][:, half * QCH + o2: half * QCH + QCH],
                            start=(ki == 0), stop=(ki == nkt2 - 1))
                    p_free[gpar] = kb.inc("te", ins)

                def emit_n2(ph, pcpar, prdy):
                    nonlocal psf_free, ctxn_rd, fld_free
                    pc, phb = ph // 2, ph % 2
                    kb.wait("te", "act", prdy)
                    kb.wait("te", "act", psf_free)
                    kb.wait("te", "ve", o_free)
                    ins = TE.matmul(ps_f, ones_row_f[0:1, 0:64], recip[pcpar][:],
                                    start=True, stop=True)
                    sums_free[pcpar] = mm_f = kb.inc("te", ins)
                    kb.wait("act", "te", mm_f)
                    kb.wait("act", "ve", fld_free)
                    ins = ACT.copy(fld[:], ps_f)
                    psf_free = kb.inc("act", ins)
                    kb.wait("ve", "act", psf_free)
                    kb.wait("ve", "te", ctxn_rd)
                    if phb == 0:
                        ins = VE.tensor_mul(ctxn[0:64, pc * QCH:(pc + 1) * QCH],
                                            ps_c[pcpar][0:64, :], fld[:])
                        fld_free = ctx_free[pcpar] = kb.inc("ve", ins)
                    else:
                        kb.wait("ve", "dsp", st_rd[pcpar])
                        ins = VE.tensor_mul(ctxn_st[:, pcpar * QCH:(pcpar + 1) * QCH],
                                            ps_c[pcpar][0:64, :], fld[:])
                        fld_free = ctx_free[pcpar] = kb.inc("ve", ins)
                        kb.wait("sp", "ve", ctx_free[pcpar])
                        kb.wait("sp", "te", ctxn_rd)
                        kb.wait("sp", "dsp", kb.cnt["dsp"])
                        st_rd[pcpar] = kb.dma("dsp", ctxn[64:128, pc * QCH:(pc + 1) * QCH],
                                              ctxn_st[:, pcpar * QCH:(pcpar + 1) * QCH])

                for qc in range(NQC):
                    nkt = 4 * qc + 4
                    for h in range(H):
                        c, hb = h // 2, h % 2
                        cpar = h % 2
                        groups = []
                        kk = 0
                        while kk < nkt:
                            o_k = max(qc * QCH, kk * 128) - qc * QCH
                            o_n = (max(qc * QCH, (kk + 1) * 128) - qc * QCH) if kk + 1 < nkt else 1
                            if o_k == 0 and o_n == 0:
                                groups.append((kk, kk + 1)); kk += 2
                            else:
                                groups.append((kk,)); kk += 1
                        prev = None
                        for gidx, grp in enumerate(groups):
                            gpar = gidx % 2
                            kb.wait("te", "act", s_free[gpar])
                            for half, ki in enumerate(grp):
                                qlo = max(qc * QCH, ki * 128)
                                o = qlo - qc * QCH
                                ins = TE.matmul(
                                    ps_s[gpar][:, half * QCH + o: (half + 1) * QCH],
                                    kt_sb[64 * hb:64 * hb + 64, c * T + ki * 128: c * T + (ki + 1) * 128],
                                    qt_sb[64 * hb:64 * hb + 64, c * T + qlo: c * T + qc * QCH + QCH],
                                    start=True, stop=True)
                            mm_s = kb.inc("te", ins)
                            kb.wait("act", "te", mm_s)
                            kb.wait("act", "te", p_free[gpar])
                            if len(grp) == 2:
                                ins = ACT.activation(p_sb[gpar][:], ps_s[gpar][:], AF.Exp,
                                                     bias=bias2[:, 2 * c + hb: 2 * c + hb + 1], scale=0.125)
                            else:
                                o = max(qc * QCH, grp[0] * 128) - qc * QCH
                                ins = ACT.activation(p_sb[gpar][:, o:QCH], ps_s[gpar][:, o:QCH], AF.Exp,
                                                     bias=bias2[:, 2 * c + hb: 2 * c + hb + 1], scale=0.125)
                            s_free[gpar] = kb.inc("act", ins)
                            mask_mark = None
                            if len(grp) == 1 and grp[0] >= 4 * qc:
                                o = max(qc * QCH, grp[0] * 128) - qc * QCH
                                kb.wait("ve", "act", s_free[gpar])
                                ins = VE.tensor_mul(p_sb[gpar][:, o:o + 128],
                                                    p_sb[gpar][:, o:o + 128], maskk[:])
                                mask_mark = ("ve", kb.inc("ve", ins))
                            if prev is not None:
                                emit_ctx_group(qc, h, *prev)
                            prev = (grp, gpar, s_free[gpar], mask_mark)
                        emit_ctx_group(qc, h, *prev)
                        mm_ctx = kb.cnt["te"]
                        kb.wait("act", "te", mm_ctx)
                        m_s = kb.inc("act", ACT.copy(sums65[cpar][64:65, :], ps_c[cpar][64:65, :]))
                        kb.wait("act", "act", m_s)
                        kb.wait("act", "te", sums_free[cpar])
                        sums_dma[cpar] = kb.dma("dsc", recip[cpar][:], sums65[cpar][64:65, :])
                        kb.wait("act", "dsc", sums_dma[cpar])
                        kb.fence("act", ACT.activation(recip[cpar][:], recip[cpar][:], AF.Ln))
                        ins = ACT.activation(recip[cpar][:], recip[cpar][:], AF.Exp, scale=-1.0)
                        rdy = kb.inc("act", ins)
                        if dbg and s == 0 and ((dbg == "b") == is_b) and qc == 3 and h == 0:
                            kb.wait("ve", "act", rdy)
                            kb.inc("ve", VE.tensor_copy(dbg_rec[:], recip[cpar][:]))
                        if not DEFER_N2:
                            emit_n2(h, cpar, rdy)
                        else:
                            if pend is not None:
                                emit_n2(*pend)
                            pend = (h, cpar, rdy)
                    if pend is not None:
                        emit_n2(*pend)
                    pend = None
                    kb.wait("te", "ve", kb.cnt["ve"])
                    kb.wait("te", "dsp", kb.cnt["dsp"])
                    kb.wait("te", "act", psf_free)
                    for tt in range(4):
                        ti = qc * 4 + tt
                        kb.wait("te", "ve", o_free)
                        for pr in range(NP):
                            for eh in range(2):
                                ins = TE.matmul(
                                    ps_o[:, eh * 512:(eh + 1) * 512],
                                    ctxn[:, pr * QCH + tt * 128: pr * QCH + (tt + 1) * 128],
                                    wo_sb[:, pr * 1024 + eh * 512: pr * 1024 + (eh + 1) * 512],
                                    start=(pr == 0), stop=(pr == NP - 1 and not is_b))
                                if is_b and pr == NP - 1:
                                    ins = TE.matmul(ps_o[:, eh * 512:(eh + 1) * 512],
                                                    ones_row[:], borow[:, eh * 512:(eh + 1) * 512],
                                                    start=False, stop=True)
                        mm_o = kb.inc("te", ins)
                        kb.wait("ve", "te", mm_o)
                        xt = x_sb[:, ti * D:(ti + 1) * D]
                        ins = VE.tensor_add(xt, ps_o[:], xt)
                        o_free = kb.inc("ve", ins)
                    ctxn_rd = kb.cnt["te"]
                    if dbg and s == 0 and ((dbg == "b") == is_b) and qc == 3:
                        kb.join(("sp",))
                        kb.dma("dsp", dbg_outs["d_ctxn"], ctxn[:])
                        kb.dma("dsp", dbg_outs["d_p15"], dbg_p15[:])
                        kb.dma("dsp", dbg_outs["d_bias2"], bias2[:])
                        kb.dma("dsp", dbg_outs["d_s15"], dbg_s15[:])
                        kb.dma("dsp", dbg_outs["d_rec"], dbg_rec[:])
                        kb.wait("sp", "dsp", kb.cnt["dsp"])
                kb.join()

        def _unused():
            pass

    def emit_ffn(s):
        with ExitStack() as fsc:
            fsb = lambda n, sh, dt: fsc.enter_context(nc.sbuf_tensor(uname(n), sh, dt))
            ln2t = fsb("ln2t", [128, EC * T], BF16)
            mk_bd = fsb("mk_bd2", [128, 32 * 128], BF16)
            up_bd = fsb("up_bd2", [128, 64 * 128], BF16)
            kb.dma("dsp", mk_bd[:], rails_dram[:, 0:32 * 128])
            c_rails = kb.dma("dsp", up_bd[:], rails_dram[:, 32 * 128:96 * 128])
            with ExitStack() as psc:
                sc = ln_scratch(psc, "_ff")
                ps_t = [psc.enter_context(nc.psum_tensor(uname(f"ps_t2{i}"), [128, 512], BF16)) for i in range(2)]
                ps_sq = psc.enter_context(nc.psum_tensor(uname("ps_sq2"), [128, 2048], F32))
                emit_ln(ln2t, sc, ps_t, [ps_sq[:, 0:512], ps_sq[:, 512:1024],
                                         ps_sq[:, 1024:1536], ps_sq[:, 1536:2048]])
                kb.join()
            wct = [fsb(f"wct{i}", [128, SUPC * 1024], BF16) for i in range(2)]
            hsup = fsb("hsup", [128, SUPC * QCH], BF16)
            ysb = [fsb(f"ysb{i}", [128, QCH], BF16) for i in range(2)]
            ps_y = [fsc.enter_context(nc.psum_tensor(uname(f"ps_y{i}"), [128, QCH], F32)) for i in range(2)]
            ps_z = [fsc.enter_context(nc.psum_tensor(uname(f"ps_z{i}"), [128, 1024], F32)) for i in range(2)]
            ps_o = fsc.enter_context(nc.psum_tensor(uname("ps_o2"), [128, 1024], F32))

            wct_rdy = [0, 0]
            wct_free = [0, 0]
            wct_rdy[0] = kb.dma("dsp", wct[0][:].rearrange("p (ch e) -> p ch e", ch=SUPC),
                                wct_dram[0:SUPF, :].rearrange("(ch p) e -> p ch e", p=128))
            if dbg:
                kb.join(("sp",))
                kb.dma("dsp", dbg_outs["d_wct"], wct_dram[:])
                kb.dma("dsp", dbg_outs["d_ln2t"], ln2t[:])
                kb.dma("dsp", dbg_outs["d_mkbd"], mk_bd[:])
                kb.dma("dsp", dbg_outs["d_upbd"], up_bd[:])
                kb.wait("sp", "dsp", kb.cnt["dsp"])
                kb.join()
            ysb_free = [0, 0]
            psy_free = [0, 0]
            z_free = [0, 0]
            o_free = 0
            kb.wait("te", "dsp", c_rails)
            for sup in range(NSUP):
                wpar = sup % 2
                if sup + 1 < NSUP:
                    npar = (sup + 1) % 2
                    kb.wait("sp", "te", wct_free[npar])
                    kb.wait("sp", "dsp", kb.cnt["dsp"])
                    wct_rdy[npar] = kb.dma("dsp", wct[npar][:].rearrange("p (ch e) -> p ch e", ch=SUPC),
                                           wct_dram[(sup + 1) * SUPF:(sup + 2) * SUPF, :].rearrange(
                                               "(ch p) e -> p ch e", p=128))
                h = sup // 2
                gs = [4 * (sup % 2) + i for i in range(4)]
                kb.wait("te", "dsp", wct_rdy[wpar])
                for t5 in range(4):
                    def emit_y(gi):
                        g = gs[gi]
                        ypar = gi % 2
                        kb.wait("te", "ve", psy_free[ypar])
                        ins = TE.matmul(ps_y[ypar][:],
                                        mk_bd[:, (h * 8 + g) * 128:(h * 8 + g + 1) * 128],
                                        ln2t[:, g * T + t5 * QCH: g * T + (t5 + 1) * QCH],
                                        start=True, stop=True)
                        return kb.inc("te", ins)
                    mm_ys = [emit_y(0)]
                    for gi, g in enumerate(gs):
                        ypar = gi % 2
                        kb.wait("ve", "te", mm_ys[gi])
                        kb.wait("ve", "te", ysb_free[ypar])
                        ins = VE.tensor_copy(ysb[ypar][:], ps_y[ypar][:])
                        psy_free[ypar] = kb.inc("ve", ins)
                        if gi + 1 < 4:
                            mm_ys.append(emit_y(gi + 1))
                        kb.wait("te", "ve", psy_free[ypar])
                        for zh in range(2):
                            zpar = zh
                            kb.wait("te", "act", z_free[zpar])
                            for half in range(2):
                                ch4 = zh * 2 + half
                                a = ch4 // 2
                                slot = h * 16 + g * 2 + (ch4 % 2)
                                ins = TE.matmul(ps_z[zpar][:, half * QCH:(half + 1) * QCH],
                                                up_bd[64 * a:64 * a + 64, slot * 128:(slot + 1) * 128],
                                                ysb[ypar][64 * a:64 * a + 64, :],
                                                start=True, stop=True)
                            mm_z = kb.inc("te", ins)
                            kb.wait("act", "te", mm_z)
                            chl = gi * 4 + zh * 2
                            ins = ACT.activation(hsup[:, chl * QCH:(chl + 2) * QCH], ps_z[zpar][:], AF.Gelu)
                            z_free[zpar] = kb.inc("act", ins)
                        ysb_free[ypar] = kb.cnt["te"]
                    m_h = kb.cnt["act"]
                    kb.wait("te", "act", m_h)
                    if dbg and sup == 0 and t5 == 0:
                        kb.join(("sp",))
                        kb.dma("dsp", dbg_outs["d_hsup"], hsup[:])
                        kb.wait("sp", "dsp", kb.cnt["dsp"])
                        kb.join()
                    for tt in range(4):
                        ti = t5 * 4 + tt
                        kb.wait("te", "ve", o_free)
                        for chl in range(SUPC):
                            for eh in range(2):
                                ins = TE.matmul(
                                    ps_o[:, eh * 512:(eh + 1) * 512],
                                    hsup[:, chl * QCH + tt * 128: chl * QCH + (tt + 1) * 128],
                                    wct[wpar][:, chl * 1024 + eh * 512: chl * 1024 + (eh + 1) * 512],
                                    start=(chl == 0), stop=(chl == SUPC - 1))
                        mm_o = kb.inc("te", ins)
                        kb.wait("ve", "te", mm_o)
                        xt = x_sb[:, ti * D:(ti + 1) * D]
                        ins = VE.tensor_add(xt, ps_o[:], xt)
                        o_free = kb.inc("ve", ins)
                wct_free[wpar] = kb.cnt["te"]
            kb.join()

    for s in range(n_seq):
        c_x = kb.dma("dsp", x_sb[:].rearrange("p (ti e) -> p ti e", ti=TTILES),
                      x_in[s].rearrange("(ti p) e -> p ti e", p=128))
        for e in ("te", "ve", "act", "pool"):
            kb.wait(e, "dsp", c_x)
        if "a" in stages:
            emit_attn(s, "a")
        if "f1" in stages:
            emit_ffn(s)
        if "b" in stages:
            emit_attn(s, "b")
        if "f2" in stages:
            emit_ffn(s)
        kb.join()
        kb.fence("ve", VE.tensor_copy(x_sb[:, 0:8], x_sb[:, 0:8]))
        kb.join(("sp",))
        kb.dma("dsp", x_out[s].rearrange("(ti p) e -> p ti e", p=128),
               x_sb[:].rearrange("p (ti e) -> p ti e", ti=TTILES))
    kb.wait("sp", "dsp", kb.cnt["dsp"])
    ctx.close()
    return nc


def kernel(**inputs):
    x = inputs["L0_attn_a_x"]
    nc = build_program()
    in_maps = []
    for c in range(NCORES):
        m = {"L0_attn_a_x": np.ascontiguousarray(x[SPC * c:SPC * (c + 1)])}
        for k, v in inputs.items():
            if k != "L0_attn_a_x":
                m[k] = v
        in_maps.append(m)
    res = run_bass_kernel_spmd(nc, in_maps, core_ids=list(range(NCORES)))
    return np.concatenate([res.results[c]["out"] for c in range(NCORES)], axis=0)


# version 13 (final)
# speedup vs baseline: 1.4708x; 1.0331x over previous
import numpy as np
from contextlib import ExitStack

import concourse.bass as bass
import concourse.mybir as mybir
from concourse.bass_utils import run_bass_kernel_spmd

F32 = mybir.dt.float32
BF16 = mybir.dt.bfloat16
AF = mybir.ActivationFunctionType
ALU = mybir.AluOpType
AX = mybir.AxisListType

B, T, D = 16, 2048, 1024
NCORES = 8
SPC = B // NCORES
H, DK = 16, 64
NP = H // 2
EC = D // 128
TTILES = T // 128
QCH = 512
NQC = T // QCH
FNH, RAILS, RW, FMULT = 4, 64, 16, 4
F_TOT = FNH * RAILS * RW * FMULT
SUPF = 2048
NSUP = F_TOT // SUPF
SUPC = SUPF // 128
LN_EPS = 1e-5
DEFER_N2 = True
PIPE_KI = True
MASK_ON_POOL = False
PAIR_JOIN = False

ALL_SEMS = ("te", "ve", "act", "pool", "dsp", "dgp", "dsc")


class KB:
    def __init__(self, nc, ctx):
        self.nc = nc
        self.sems, self.cnt = {}, {}
        for name in ALL_SEMS:
            self.sems[name] = ctx.enter_context(nc.semaphore("s_" + name))
            self.cnt[name] = 0
        self.engs = {"te": nc.tensor, "ve": nc.vector, "act": nc.scalar,
                     "pool": nc.gpsimd, "sp": nc.sync}

    def inc(self, name, instr, k=1):
        instr.then_inc(self.sems[name], k)
        self.cnt[name] += k
        return self.cnt[name]

    def dma(self, qname, out, in_, **kw):
        eng = {"dsp": self.nc.sync, "dgp": self.nc.gpsimd, "dsc": self.nc.scalar}[qname]
        return self.inc(qname, eng.dma_start(out=out, in_=in_, **kw), 16)

    def wait(self, eng_name, sem_name, val=None):
        v = self.cnt[sem_name] if val is None else val
        if v > 0:
            self.engs[eng_name].wait_ge(self.sems[sem_name], v)

    def join(self, engines=("te", "ve", "act", "pool", "sp")):
        for e in engines:
            for s in ALL_SEMS:
                self.wait(e, s)

    def fence(self, name, instr):
        m = self.inc(name, instr)
        self.engs[name].wait_ge(self.sems[name], m)
        return m

    def snap(self):
        return dict(self.cnt)


def build_program(stages=("a", "f1", "b", "f2"), n_seq=SPC, dbg=False):
    if dbg is True:
        dbg = "a"
    nc = bass.Bass("TRN2", target_bir_lowering=False, debug=False)

    x_in = nc.dram_tensor("L0_attn_a_x", [n_seq, T, D], F32, kind="ExternalInput").ap()
    wq_a = nc.dram_tensor("L0_attn_a_Wq", [D, D], F32, kind="ExternalInput").ap()
    wk_a = nc.dram_tensor("L0_attn_a_Wk", [D, D], F32, kind="ExternalInput").ap()
    wv_a = nc.dram_tensor("L0_attn_a_Wv", [D, D], F32, kind="ExternalInput").ap()
    wo_a = nc.dram_tensor("L0_attn_a_Wo", [D, D], F32, kind="ExternalInput").ap()
    wq_b = nc.dram_tensor("L0_attn_b_Wq", [H, D, DK], F32, kind="ExternalInput").ap()
    bq_b = nc.dram_tensor("L0_attn_b_bq", [H, DK], F32, kind="ExternalInput").ap()
    wk_b = nc.dram_tensor("L0_attn_b_Wk", [H, D, DK], F32, kind="ExternalInput").ap()
    bk_b = nc.dram_tensor("L0_attn_b_bk", [H, DK], F32, kind="ExternalInput").ap()
    wv_b = nc.dram_tensor("L0_attn_b_Wv", [H, D, DK], F32, kind="ExternalInput").ap()
    bv_b = nc.dram_tensor("L0_attn_b_bv", [H, DK], F32, kind="ExternalInput").ap()
    wo_b = nc.dram_tensor("L0_attn_b_Wo", [D, D], F32, kind="ExternalInput").ap()
    bo_b = nc.dram_tensor("L0_attn_b_bo", [D], F32, kind="ExternalInput").ap()
    mkr = nc.dram_tensor("L0_ffn_rail_maker", [FNH, RAILS, RW, RW], F32, kind="ExternalInput").ap()
    upj = nc.dram_tensor("L0_ffn_rail_up_proj", [FNH, RAILS, RW * FMULT, RW], F32, kind="ExternalInput").ap()
    wc = nc.dram_tensor("L0_ffn_w_cproj", [D, F_TOT], F32, kind="ExternalInput").ap()
    x_out = nc.dram_tensor("out", [n_seq, T, D], F32, kind="ExternalOutput").ap()

    dbg_outs = {}
    if dbg:
        for nm, sh, dt in (("d_lnt", [128, EC * T], BF16), ("d_qt", [128, NP * T], BF16),
                           ("d_kt", [128, NP * T], BF16), ("d_vp", [128, TTILES * H * 65], BF16),
                           ("d_bias2", [128, 2 * NP], F32), ("d_ctxn", [128, NP * QCH], BF16),
                           ("d_wct", [F_TOT, D], BF16), ("d_hsup", [128, SUPC * QCH], BF16),
                           ("d_ln2t", [128, EC * T], BF16),
                           ("d_mkbd", [128, 32 * 128], BF16), ("d_upbd", [128, 64 * 128], BF16),
                           ("d_p15", [128, 128], BF16), ("d_rec", [1, QCH], F32),
                           ("d_s15", [128, 128], F32)):
            dbg_outs[nm] = nc.dram_tensor(nm, sh, dt, kind="ExternalOutput").ap()
    wct_dram = nc.dram_tensor("wct_scratch", [F_TOT, D], BF16, kind="Internal").ap()
    rails_dram = nc.dram_tensor("rails_scratch", [128, 96 * 128], BF16, kind="Internal").ap()

    ctx = ExitStack()
    kb = KB(nc, ctx)
    TE, VE, ACT, POOL, SP = nc.tensor, nc.vector, nc.scalar, nc.gpsimd, nc.sync

    _ncnt = [0]

    def uname(n):
        _ncnt[0] += 1
        return f"{n}_{_ncnt[0]}"

    def sb(name, shape, dt):
        return ctx.enter_context(nc.sbuf_tensor(name, shape, dt))

    x_sb = sb("x_sb", [128, TTILES * D], F32)
    ident = sb("ident", [128, 128], BF16)
    maskk = sb("maskk", [128, 128], BF16)
    ones_row = sb("ones_row", [1, 128], BF16)
    ones_row_f = sb("ones_row_f", [1, 128], F32)
    negeighth = sb("negeighth", [1, 128], BF16)
    ind2 = sb("ind2", [128, 2], BF16)
    eps_col = sb("eps_col", [128, 1], F32)

    kb.fence("pool", POOL.memset(ident[:], 0.0))
    kb.fence("pool", POOL.affine_select(out=ident[:], in_=ident[:], compare_op=ALU.not_equal,
                                        fill=1.0, base=0, pattern=[[-1, 128]], channel_multiplier=1))
    kb.fence("pool", POOL.memset(maskk[:], 1.0))
    kb.fence("pool", POOL.affine_select(out=maskk[:], in_=maskk[:], compare_op=ALU.is_ge,
                                        fill=0.0, base=0, pattern=[[1, 128]], channel_multiplier=-1))
    kb.fence("pool", POOL.memset(ones_row[:], 1.0))
    kb.fence("pool", POOL.memset(ones_row_f[:], 1.0))
    kb.fence("pool", POOL.memset(negeighth[:], -0.125))
    kb.fence("pool", POOL.memset(ind2[:], 0.0))
    kb.fence("pool", POOL.memset(ind2[0:64, 0:1], 1.0))
    kb.fence("pool", POOL.memset(eps_col[:], LN_EPS))
    kb.fence("pool", POOL.memset(ind2[64:128, 1:2], 1.0))
    kb.join()

    with ExitStack() as ph:
        wcstage = [ph.enter_context(nc.sbuf_tensor(f"wcstage{i}", [128, 8 * 512], BF16)) for i in range(2)]
        wcout = [ph.enter_context(nc.sbuf_tensor(f"wcout{i}", [128, 4 * 1024], BF16)) for i in range(2)]
        ps_w = [ph.enter_context(nc.psum_tensor(f"ps_w{i}", [128, 1024], BF16)) for i in range(2)]
        stage_free = [0, 0]
        out_free = [0, 0]
        ps_free = [0, 0]
        for fw in range(32):
            par = fw % 2
            kb.wait("pool", "te", stage_free[par])
            kb.wait("pool", "dgp", kb.cnt["dgp"])
            for ecb in range(8):
                kb.dma("dgp", wcstage[par][:, ecb * 512:(ecb + 1) * 512],
                       wc[ecb * 128:(ecb + 1) * 128, fw * 512:(fw + 1) * 512])
            stage_rdy = kb.cnt["dgp"]
            kb.wait("te", "dgp", stage_rdy)
            kb.wait("te", "dsp", out_free[par])
            for fb in range(4):
                pp = fb % 2
                kb.wait("te", "act", ps_free[pp])
                for ecb in range(8):
                    ins = TE.transpose(ps_w[pp][:, ecb * 128:(ecb + 1) * 128],
                                       wcstage[par][:, ecb * 512 + fb * 128: ecb * 512 + (fb + 1) * 128],
                                       ident[:])
                mark = kb.inc("te", ins)
                kb.wait("act", "te", mark)
                ps_free[pp] = kb.inc("act", ACT.copy(wcout[par][:, fb * 1024:(fb + 1) * 1024], ps_w[pp][:]))
            stage_free[par] = kb.cnt["te"]
            kb.wait("sp", "act", kb.cnt["act"])
            kb.wait("sp", "dsp", kb.cnt["dsp"])
            out_free[par] = kb.dma(
                "dsp",
                wct_dram[fw * 512:(fw + 1) * 512, :].rearrange("(fb p) e -> p fb e", p=128),
                wcout[par][:].rearrange("p (fb e) -> p fb e", fb=4))

    kb.join()

    with ExitStack() as ph:
        up_kj = ph.enter_context(nc.sbuf_tensor("up_kj", [64, 4096], BF16))
        uptall = ph.enter_context(nc.sbuf_tensor("uptall", [128, 32 * 64], BF16))
        mk_bd = ph.enter_context(nc.sbuf_tensor("mk_bd", [128, 32 * 128], BF16))
        up_bd = ph.enter_context(nc.sbuf_tensor("up_bd", [128, 64 * 128], BF16))
        ps_r = ph.enter_context(nc.psum_tensor("ps_r", [128, 512], BF16))
        kb.fence("pool", POOL.memset(mk_bd[:], 0.0))
        c_zero = kb.fence("pool", POOL.memset(up_bd[:], 0.0))
        for hh in range(FNH):
            c_up = kb.dma("dgp", up_kj[:, hh * 1024:(hh + 1) * 1024].rearrange("k (r j) -> k r j", r=64),
                          upj[hh].rearrange("r k j -> k r j"))
        kb.wait("te", "dgp", c_up)
        for h in range(FNH):
            for g in range(8):
                grp = h * 8 + g
                pp = grp % 4
                ins = TE.transpose(ps_r[:, pp * 128: pp * 128 + 64],
                                   up_kj[:, (h * 64 + 8 * g) * 16:(h * 64 + 8 * g) * 16 + 128],
                                   ident[0:64, 0:64])
                mark = kb.inc("te", ins)
                if pp == 3:
                    kb.wait("act", "te", mark)
                    lo = grp - 3
                    ins = ACT.copy(
                        uptall[:, lo * 64:(grp + 1) * 64],
                        ps_r[:].rearrange("p (q e) -> p q e", q=4)[:, :, 0:64])
                    evac = kb.inc("act", ins)
                    kb.wait("te", "act", evac)
        c_upt = kb.cnt["act"]
        kb.wait("sp", "act", c_upt)
        kb.wait("sp", "pool", c_zero)
        for h in range(FNH):
            for g in range(8):
                grp = h * 8 + g
                for rr in range(8):
                    ppair = rr // 2
                    a, bb, kr = ppair // 2, ppair % 2, rr % 2
                    slot = h * 16 + g * 2 + bb
                    kb.dma("dsp",
                           up_bd[64 * a + 32 * bb + 16 * kr: 64 * a + 32 * bb + 16 * kr + 16,
                                 slot * 128 + 64 * kr: slot * 128 + 64 * kr + 64],
                           uptall[16 * rr:16 * rr + 16, grp * 64:(grp + 1) * 64])
        kb.wait("pool", "dgp", kb.cnt["dgp"])
        for h in range(FNH):
            for r in range(RAILS):
                g, rl = r // 8, r % 8
                kb.dma("dgp",
                       mk_bd[16 * rl:16 * rl + 16,
                             (h * 8 + g) * 128 + 16 * rl:(h * 8 + g) * 128 + 16 * rl + 16],
                       mkr[h, r])
        kb.wait("sp", "dsp", kb.cnt["dsp"])
        kb.wait("sp", "dgp", kb.cnt["dgp"])
        kb.dma("dsp", rails_dram[:, 0:32 * 128], mk_bd[:])
        kb.dma("dsp", rails_dram[:, 32 * 128:96 * 128], up_bd[:])
        kb.join()

    def emit_ln(lnt, sc, ps_t, ps_sqs):
        ln_bf = sc["ln_bf"]
        st = sc["st"]
        NT = TTILES
        nm_a = st[:, 0 * NT:1 * NT]
        s2a = st[:, 1 * NT:2 * NT]
        s2b = st[:, 2 * NT:3 * NT]
        var_a = st[:, 3 * NT:4 * NT]
        rstd_a = st[:, 4 * NT:5 * NT]
        nmr_a = st[:, 5 * NT:6 * NT]
        sq_free = [0, 0]
        for ti in range(TTILES):
            xt = x_sb[:, ti * D:(ti + 1) * D]
            VE.tensor_reduce(nm_a[:, ti:ti + 1], xt, AX.X, ALU.add, negate=True)
            pp = ti % 2
            kb.wait("act", "ve", sq_free[pp])
            ins = ACT.activation(ps_sqs[2 * pp + 0], xt[:, 0:512], AF.Square)
            ins = ACT.activation(ps_sqs[2 * pp + 1], xt[:, 512:1024], AF.Square)
            m_sq = kb.inc("act", ins)
            kb.wait("ve", "act", m_sq)
            VE.tensor_reduce(s2a[:, ti:ti + 1], ps_sqs[2 * pp + 0], AX.X, ALU.add)
            ins = VE.tensor_reduce(s2b[:, ti:ti + 1], ps_sqs[2 * pp + 1], AX.X, ALU.add)
            sq_free[pp] = kb.inc("ve", ins)
        kb.wait("ve", "ve", kb.cnt["ve"])
        kb.fence("ve", VE.tensor_add(s2a, s2a, s2b))
        kb.fence("ve", VE.tensor_scalar(var_a, s2a, 1.0 / D, None, ALU.mult))
        kb.fence("ve", VE.tensor_scalar(nm_a, nm_a, 1.0 / D, None, ALU.mult))
        kb.fence("ve", VE.tensor_mul(s2b, nm_a, nm_a))
        m_v = kb.fence("ve", VE.tensor_sub(var_a, var_a, s2b))
        kb.wait("act", "ve", m_v)
        kb.fence("act", ACT.activation(s2a, var_a, AF.Ln, bias=eps_col[:]))
        m_r = kb.fence("act", ACT.activation(rstd_a, s2a, AF.Exp, scale=-0.5))
        kb.wait("ve", "act", m_r)
        m_c = kb.fence("ve", VE.tensor_mul(nmr_a, nm_a, rstd_a))
        ps_free = [0, 0]
        lnbf_free = [0, 0]
        for ti in range(TTILES):
            xt = x_sb[:, ti * D:(ti + 1) * D]
            bp = ti % 2
            kb.wait("ve", "te", lnbf_free[bp])
            ins = VE.tensor_scalar(ln_bf[:, bp * D:(bp + 1) * D], xt,
                                   rstd_a[:, ti:ti + 1], nmr_a[:, ti:ti + 1],
                                   ALU.mult, ALU.add)
            m_ln = kb.inc("ve", ins)
            kb.wait("te", "ve", m_ln)
            for half in range(2):
                pp = half
                kb.wait("te", "act", ps_free[pp])
                for cc in range(4):
                    c = half * 4 + cc
                    ins = TE.transpose(ps_t[pp][:, cc * 128:(cc + 1) * 128],
                                       ln_bf[:, bp * D + c * 128: bp * D + (c + 1) * 128], ident[:])
                m_t = kb.inc("te", ins)
                kb.wait("act", "te", m_t)
                ins = ACT.copy(
                    lnt[:].rearrange("p (c t) -> p c t", c=EC)[:, half * 4:(half + 1) * 4,
                                                              ti * 128:(ti + 1) * 128],
                    ps_t[pp][:].rearrange("p (c t) -> p c t", c=4))
                ps_free[pp] = kb.inc("act", ins)
            lnbf_free[bp] = kb.cnt["te"]

    def ln_scratch(scope, tag):
        f = lambda n, sh, dt: scope.enter_context(nc.sbuf_tensor(uname(n + tag), sh, dt))
        return {"ln_bf": f("ln_bf", [128, 2 * D], BF16),
                "st": f("st", [128, 6 * TTILES], F32)}

    def emit_attn(s, which):
        is_b = which == "b"
        with ExitStack() as lsc:
            lsb = lambda n, sh, dt: lsc.enter_context(nc.sbuf_tensor(uname(n), sh, dt))
            qt_sb = lsb("qt_sb", [128, NP * T], BF16)
            kt_sb = lsb("kt_sb", [128, NP * T], BF16)
            vp_sb = lsb("vp_sb", [128, TTILES * H * 65], BF16)
            bias2 = lsb("bias2", [128, 2 * NP], F32)
            mrow = lsb("mrow", [1, 2 * NP], BF16)

            with ExitStack() as bsc:
                bsb = lambda n, sh, dt: bsc.enter_context(nc.sbuf_tensor(uname(n), sh, dt))
                lnt = bsb("lnt", [128, EC * T], BF16)
                ps_t = [bsc.enter_context(nc.psum_tensor(uname(f"ps_t{i}"), [128, 512], BF16)) for i in range(2)]
                ps_qk = [bsc.enter_context(nc.psum_tensor(uname(f"ps_qk{i}"), [128, 512], F32)) for i in range(2)]
                ps_v = [bsc.enter_context(nc.psum_tensor(uname(f"ps_v{i}"), [128, 512], F32)) for i in range(2)]
                ps_n = [bsc.enter_context(nc.psum_tensor(uname(f"ps_n{i}"), [2, 512], F32)) for i in range(2)]
                with ExitStack() as lns:
                    sc = ln_scratch(lns, "_at")
                    emit_ln(lnt, sc, ps_t,
                            [ps_qk[0][:], ps_qk[1][:], ps_v[0][:], ps_v[1][:]])
                c_ln = kb.cnt["act"]
                c_lnv = kb.cnt["ve"]
                c_lnte = kb.cnt["te"]
                wstage = bsb("wstage", [128, 3 * 1024], BF16)
                bstage = bsb("bstage", [128, 4], F32)
                bvrow = bsb("bvrow", [1, 256], BF16)
                q2s = bsb("q2s", [128, 2 * 512], BF16)
                nred = bsb("nred", [2, 8], F32)
                nq_all = bsb("nq_all", [2, 2 * NP], F32)
                mh_all = bsb("mh_all", [2, NP], BF16)

                kb.wait("pool", "ve", c_lnv)
                kb.wait("pool", "te", c_lnte)
                kb.wait("sp", "ve", c_lnv)
                kb.wait("sp", "te", c_lnte)
                kb.fence("pool", POOL.memset(vp_sb[:], 1.0))
                c_vones = kb.fence("pool", POOL.memset(bias2[:], 0.0))
                kb.wait("ve", "pool", c_vones)
                kb.wait("te", "act", c_ln)

                def load_pair_w(c):
                    par = c % 2
                    if not is_b:
                        kb.dma("dgp", wstage[:, 0:1024].rearrange("p (ec d) -> p ec d", ec=EC),
                               wq_a[:, c * 128:(c + 1) * 128].rearrange("(ec p) d -> p ec d", p=128))
                        kb.dma("dgp", wstage[:, 1024:2048].rearrange("p (ec d) -> p ec d", ec=EC),
                               wk_a[:, c * 128:(c + 1) * 128].rearrange("(ec p) d -> p ec d", p=128))
                        cw = kb.dma("dgp", wstage[:, 2048:3072].rearrange("p (ec d) -> p ec d", ec=EC),
                                    wv_a[:, c * 128:(c + 1) * 128].rearrange("(ec p) d -> p ec d", p=128))
                        cb = 0
                    else:
                        for wi, wsrc in ((0, wq_b), (1, wk_b), (2, wv_b)):
                            for hh in range(2):
                                kb.dma("dgp",
                                       wstage[:, wi * 1024:(wi + 1) * 1024].rearrange(
                                           "p (ec x) -> p ec x", ec=EC)[:, :, hh * 64:(hh + 1) * 64],
                                       wsrc[2 * c + hh].rearrange("(ec p) d -> p ec d", p=128))
                        cw = kb.dma("dgp", bvrow[0:1, par * 128:(par + 1) * 128],
                                    bv_b[2 * c:2 * c + 2].rearrange("h d -> (h d)").rearrange("(o x) -> o x", o=1))
                        kb.dma("dsp", bstage[:, 2 * par:2 * par + 1],
                               bq_b[2 * c:2 * c + 2].rearrange("h d -> (h d)").rearrange("(x o) -> x o", o=1))
                        cb = kb.dma("dsp", bstage[:, 2 * par + 1:2 * par + 2],
                                    bk_b[2 * c:2 * c + 2].rearrange("h d -> (h d)").rearrange("(x o) -> x o", o=1))
                    return cw, cb

                cw, cb = load_pair_w(0)
                qk_free = [0, 0]
                v_free = [0, 0]
                n_free = [0, 0]
                q2_free = [0, 0]
                for c in range(NP):
                    par = c % 2
                    kb.wait("te", "dgp", cw)
                    if is_b:
                        kb.wait("ve", "dsp", cb)
                    for qk in range(2):
                        dst = qt_sb if qk == 0 else kt_sb
                        for qc in range(NQC):
                            pp = qc % 2
                            kb.wait("te", "ve" if is_b else "act", qk_free[pp])
                            for ecb in range(EC):
                                ins = TE.matmul(
                                    ps_qk[pp][:],
                                    wstage[:, qk * 1024 + ecb * 128: qk * 1024 + (ecb + 1) * 128],
                                    lnt[:, ecb * T + qc * QCH: ecb * T + (qc + 1) * QCH],
                                    start=(ecb == 0), stop=(ecb == EC - 1))
                            mm = kb.inc("te", ins)
                            if is_b:
                                kb.wait("ve", "te", mm)
                                ins = VE.tensor_scalar(dst[:, c * T + qc * QCH: c * T + (qc + 1) * QCH],
                                                       ps_qk[pp][:], bstage[:, 2 * par + qk:2 * par + qk + 1],
                                                       None, ALU.add)
                                qk_free[pp] = kb.inc("ve", ins)
                            else:
                                kb.wait("act", "te", mm)
                                ins = ACT.copy(dst[:, c * T + qc * QCH: c * T + (qc + 1) * QCH], ps_qk[pp][:])
                                qk_free[pp] = kb.inc("act", ins)
                    for ti in range(TTILES):
                        pp = ti % 2
                        kb.wait("te", "ve", v_free[pp])
                        for ecb in range(EC):
                            ins = TE.matmul(ps_v[pp][:, 0:128],
                                            lnt[:, ecb * T + ti * 128: ecb * T + (ti + 1) * 128],
                                            wstage[:, 2048 + ecb * 128: 2048 + (ecb + 1) * 128],
                                            start=(ecb == 0), stop=(not is_b and ecb == EC - 1))
                        if is_b:
                            ins = TE.matmul(ps_v[pp][:, 0:128], ones_row[:],
                                            bvrow[0:1, par * 128:(par + 1) * 128], start=False, stop=True)
                        mm = kb.inc("te", ins)
                        kb.wait("ve", "te", mm)
                        dstv = vp_sb[:, ti * H * 65 + 2 * c * 65: ti * H * 65 + 2 * c * 65 + 130]
                        ins = VE.tensor_copy(dstv.rearrange("p (h x) -> p h x", h=2)[:, :, 0:64],
                                             ps_v[pp][:, 0:128].rearrange("p (h x) -> p h x", h=2))
                        v_free[pp] = kb.inc("ve", ins)
                    te_pair_done = kb.cnt["te"]
                    if c + 1 < NP:
                        kb.wait("pool", "te", te_pair_done)
                        kb.wait("pool", "dgp", kb.cnt["dgp"])
                        if is_b:
                            kb.wait("sp", "ve", qk_free[0])
                            kb.wait("sp", "ve", qk_free[1])
                        cw, cb = load_pair_w(c + 1)
                    for qk in range(2):
                        src = qt_sb if qk == 0 else kt_sb
                        for qc in range(NQC):
                            pp = qc % 2
                            kb.wait("act", "te", q2_free[pp])
                            kb.wait("act", "ve" if is_b else "act", qk_free[0])
                            kb.wait("act", "ve" if is_b else "act", qk_free[1])
                            ins = ACT.activation(q2s[:, pp * 512:(pp + 1) * 512],
                                                 src[:, c * T + qc * QCH: c * T + (qc + 1) * QCH], AF.Square)
                            m_sq = kb.inc("act", ins)
                            kb.wait("te", "act", m_sq)
                            kb.wait("te", "ve", n_free[pp])
                            ins = TE.matmul(ps_n[pp][:], ind2[:], q2s[:, pp * 512:(pp + 1) * 512],
                                            start=True, stop=True)
                            q2_free[pp] = mm = kb.inc("te", ins)
                            kb.wait("ve", "te", mm)
                            ins = VE.tensor_reduce(nred[:, qk * 4 + qc: qk * 4 + qc + 1], ps_n[pp][:],
                                                   AX.X, ALU.max)
                            n_free[pp] = kb.inc("ve", ins)
                    kb.wait("ve", "ve", kb.cnt["ve"])
                    kb.fence("ve", VE.tensor_reduce(nq_all[:, c:c + 1], nred[:, 0:4], AX.X, ALU.max))
                    kb.fence("ve", VE.tensor_reduce(nq_all[:, NP + c:NP + c + 1], nred[:, 4:8], AX.X, ALU.max))
                    if PAIR_JOIN:
                        kb.join()
                m_nq = kb.cnt["ve"]
                kb.wait("act", "ve", m_nq)
                kb.fence("act", ACT.activation(nq_all[:, 0:NP], nq_all[:, 0:NP], AF.Ln))
                m_l = kb.fence("act", ACT.activation(nq_all[:, NP:2 * NP], nq_all[:, NP:2 * NP], AF.Ln))
                kb.wait("ve", "act", m_l)
                m_a = kb.fence("ve", VE.tensor_add(nq_all[:, 0:NP], nq_all[:, 0:NP], nq_all[:, NP:2 * NP]))
                kb.wait("act", "ve", m_a)
                m_e = kb.inc("act", ACT.activation(mh_all[:], nq_all[:, 0:NP], AF.Exp, scale=0.5))
                kb.wait("sp", "act", m_e)
                kb.dma("dsp", mrow[:].rearrange("o (p r) -> o p r", r=2)[:, :, 0:1], mh_all[0:1, :],
                       allow_slow_non_contiguous=True)
                c_mr = kb.dma("dsp", mrow[:].rearrange("o (p r) -> o p r", r=2)[:, :, 1:2], mh_all[1:2, :],
                              allow_slow_non_contiguous=True)
                kb.join()
                if dbg and s == 0 and ((dbg == "b") == is_b):
                    kb.dma("dsp", dbg_outs["d_lnt"], lnt[:])
                    kb.dma("dsp", dbg_outs["d_qt"], qt_sb[:])
                    kb.dma("dsp", dbg_outs["d_kt"], kt_sb[:])
                    kb.wait("sp", "dsp", kb.cnt["dsp"])
                    kb.join()

            with ExitStack() as ssc:
                ssb = lambda n, sh, dt: ssc.enter_context(nc.sbuf_tensor(uname(n), sh, dt))
                wo_sb = ssb("wo_sb", [128, NP * 1024], BF16)
                ctxn = ssb("ctxn", [128, NP * QCH], BF16)
                ctxn_st = ssb("ctxn_st", [64, 2 * QCH], BF16)
                p_sb = [ssb(f"p_sb{i}", [128, 2 * QCH], BF16) for i in range(2)]
                sums65 = [ssb(f"sums65{i}", [65, QCH], F32) for i in range(2)]
                recip = [ssb(f"recip{i}", [1, QCH], F32) for i in range(2)]
                ones65 = ssb("ones65", [65, 128], F32)
                fld = ssb("fld", [64, QCH], F32)
                borow = ssb("borow", [1, 1024], BF16) if is_b else None
                wostg = ssb("wostg", [128, 1024], BF16) if is_b else None
                c_o65 = kb.fence("pool", POOL.memset(ones65[64:65, :], 1.0))
                kb.wait("te", "pool", c_o65)
                with nc.psum_tensor(uname("ps_b2"), [128, 2 * NP], F32) as ps_b2:
                    kb.wait("te", "dsp", kb.cnt["dsp"])
                    mmb = kb.inc("te", TE.matmul(ps_b2[:], negeighth[:], mrow[:], start=True, stop=True))
                    kb.wait("act", "te", mmb)
                    c_b2 = kb.inc("act", ACT.copy(bias2[:], ps_b2[:]))
                    kb.wait("act", "act", c_b2)
                if not is_b:
                    c_wo = kb.dma("dgp", wo_sb[:].rearrange("p (pq e) -> p pq e", pq=NP),
                                  wo_a.rearrange("(pq p) e -> p pq e", p=128))
                else:
                    c_bo = kb.dma("dgp", borow[:], bo_b.rearrange("(a e) -> a e", a=1))
                    with nc.psum_tensor(uname("ps_wo"), [128, 512], BF16) as ps_wo:
                        wops_free = 0
                        wostg_free = 0
                        wo_view = wo_sb[:].rearrange("p (pq e) -> p pq e", pq=NP)
                        for i in range(EC):
                            kb.wait("pool", "te", wostg_free)
                            kb.wait("pool", "dgp", kb.cnt["dgp"])
                            c_st = kb.dma("dgp", wostg[:], wo_b[i * 128:(i + 1) * 128, :])
                            kb.wait("te", "dgp", c_st)
                            for jq in range(2):
                                kb.wait("te", "act", wops_free)
                                for jj in range(4):
                                    j = jq * 4 + jj
                                    ins = TE.transpose(ps_wo[:, jj * 128:(jj + 1) * 128],
                                                       wostg[:, j * 128:(j + 1) * 128], ident[:])
                                mm = kb.inc("te", ins)
                                kb.wait("act", "te", mm)
                                ins = ACT.copy(wo_view[:, jq * 4:(jq + 1) * 4, i * 128:(i + 1) * 128],
                                               ps_wo[:].rearrange("p (q e) -> p q e", q=4))
                                wops_free = kb.inc("act", ins)
                            wostg_free = kb.cnt["te"]
                    c_wo = kb.dma("dgp", borow[:], bo_b.rearrange("(a e) -> a e", a=1))

                if dbg:
                    dbg_p15 = ssb("dbg_p15", [128, 128], BF16)
                    dbg_rec = ssb("dbg_rec", [1, QCH], F32)
                    dbg_s15 = ssb("dbg_s15", [128, 128], F32)
                ps_s = [ssc.enter_context(nc.psum_tensor(uname(f"ps_s{i}"), [128, 2 * QCH], F32)) for i in range(2)]
                ps_c = [ssc.enter_context(nc.psum_tensor(uname(f"ps_c{i}"), [65, QCH], F32)) for i in range(2)]
                ps_o = ssc.enter_context(nc.psum_tensor(uname("ps_o"), [128, 1024], F32))
                ps_f = ps_o[0:64, 0:QCH]
                kb.wait("te", "dgp", c_wo)
                kb.wait("te", "act", kb.cnt["act"])
                kb.wait("te", "ve", kb.cnt["ve"])

                ctx_free = [0, 0]
                sums_free = [0, 0]
                s_free = [0, 0]
                p_free = [0, 0]
                psf_free = 0
                fld_free = 0
                ctxn_rd = 0
                st_rd = [0, 0]
                sums_dma = [0, 0]
                o_free = 0
                pend = None

                def emit_ctx_group(qc, h, grp, gpar, sfree_val, mask_mark):
                    cpar2 = h % 2
                    nkt2 = 4 * qc + 4
                    kb.wait("te", "act", sfree_val)
                    if mask_mark is not None:
                        kb.wait("te", mask_mark[0], mask_mark[1])
                    for half, ki in enumerate(grp):
                        qlo2 = max(qc * QCH, ki * 128)
                        o2 = qlo2 - qc * QCH
                        if ki == 0:
                            kb.wait("te", "ve", ctx_free[cpar2])
                        ins = TE.matmul(
                            ps_c[cpar2][:, o2:QCH],
                            vp_sb[:, ki * H * 65 + h * 65: ki * H * 65 + (h + 1) * 65],
                            p_sb[gpar][:, half * QCH + o2: half * QCH + QCH],
                            start=(ki == 0), stop=(ki == nkt2 - 1))
                    p_free[gpar] = kb.inc("te", ins)

                def emit_n2(ph, pcpar, prdy):
                    nonlocal psf_free, ctxn_rd, fld_free
                    pc, phb = ph // 2, ph % 2
                    kb.wait("te", "act", prdy)
                    kb.wait("te", "act", psf_free)
                    kb.wait("te", "ve", o_free)
                    ins = TE.matmul(ps_f, ones_row_f[0:1, 0:64], recip[pcpar][:],
                                    start=True, stop=True)
                    sums_free[pcpar] = mm_f = kb.inc("te", ins)
                    kb.wait("act", "te", mm_f)
                    kb.wait("act", "ve", fld_free)
                    ins = ACT.copy(fld[:], ps_f)
                    psf_free = kb.inc("act", ins)
                    kb.wait("ve", "act", psf_free)
                    kb.wait("ve", "te", ctxn_rd)
                    if phb == 0:
                        ins = VE.tensor_mul(ctxn[0:64, pc * QCH:(pc + 1) * QCH],
                                            ps_c[pcpar][0:64, :], fld[:])
                        fld_free = ctx_free[pcpar] = kb.inc("ve", ins)
                    else:
                        kb.wait("ve", "dsp", st_rd[pcpar])
                        ins = VE.tensor_mul(ctxn_st[:, pcpar * QCH:(pcpar + 1) * QCH],
                                            ps_c[pcpar][0:64, :], fld[:])
                        fld_free = ctx_free[pcpar] = kb.inc("ve", ins)
                        kb.wait("sp", "ve", ctx_free[pcpar])
                        kb.wait("sp", "te", ctxn_rd)
                        kb.wait("sp", "dsp", kb.cnt["dsp"])
                        st_rd[pcpar] = kb.dma("dsp", ctxn[64:128, pc * QCH:(pc + 1) * QCH],
                                              ctxn_st[:, pcpar * QCH:(pcpar + 1) * QCH])

                for qc in range(NQC):
                    nkt = 4 * qc + 4
                    for h in range(H):
                        c, hb = h // 2, h % 2
                        cpar = h % 2
                        groups = []
                        kk = 0
                        while kk < nkt:
                            o_k = max(qc * QCH, kk * 128) - qc * QCH
                            o_n = (max(qc * QCH, (kk + 1) * 128) - qc * QCH) if kk + 1 < nkt else 1
                            if o_k == 0 and o_n == 0:
                                groups.append((kk, kk + 1)); kk += 2
                            else:
                                groups.append((kk,)); kk += 1
                        prev = None
                        for gidx, grp in enumerate(groups):
                            gpar = gidx % 2
                            kb.wait("te", "act", s_free[gpar])
                            for half, ki in enumerate(grp):
                                qlo = max(qc * QCH, ki * 128)
                                o = qlo - qc * QCH
                                ins = TE.matmul(
                                    ps_s[gpar][:, half * QCH + o: (half + 1) * QCH],
                                    kt_sb[64 * hb:64 * hb + 64, c * T + ki * 128: c * T + (ki + 1) * 128],
                                    qt_sb[64 * hb:64 * hb + 64, c * T + qlo: c * T + qc * QCH + QCH],
                                    start=True, stop=True)
                            mm_s = kb.inc("te", ins)
                            kb.wait("act", "te", mm_s)
                            kb.wait("act", "te", p_free[gpar])
                            if len(grp) == 2:
                                ins = ACT.activation(p_sb[gpar][:], ps_s[gpar][:], AF.Exp,
                                                     bias=bias2[:, 2 * c + hb: 2 * c + hb + 1], scale=0.125)
                            else:
                                o = max(qc * QCH, grp[0] * 128) - qc * QCH
                                ins = ACT.activation(p_sb[gpar][:, o:QCH], ps_s[gpar][:, o:QCH], AF.Exp,
                                                     bias=bias2[:, 2 * c + hb: 2 * c + hb + 1], scale=0.125)
                            s_free[gpar] = kb.inc("act", ins)
                            mask_mark = None
                            if len(grp) == 1 and grp[0] >= 4 * qc:
                                o = max(qc * QCH, grp[0] * 128) - qc * QCH
                                kb.wait("ve", "act", s_free[gpar])
                                ins = VE.tensor_mul(p_sb[gpar][:, o:o + 128],
                                                    p_sb[gpar][:, o:o + 128], maskk[:])
                                mask_mark = ("ve", kb.inc("ve", ins))
                            if prev is not None:
                                emit_ctx_group(qc, h, *prev)
                            prev = (grp, gpar, s_free[gpar], mask_mark)
                        emit_ctx_group(qc, h, *prev)
                        mm_ctx = kb.cnt["te"]
                        kb.wait("act", "te", mm_ctx)
                        m_s = kb.inc("act", ACT.copy(sums65[cpar][64:65, :], ps_c[cpar][64:65, :]))
                        kb.wait("act", "act", m_s)
                        kb.wait("act", "te", sums_free[cpar])
                        sums_dma[cpar] = kb.dma("dsc", recip[cpar][:], sums65[cpar][64:65, :])
                        kb.wait("act", "dsc", sums_dma[cpar])
                        kb.fence("act", ACT.activation(recip[cpar][:], recip[cpar][:], AF.Ln))
                        ins = ACT.activation(recip[cpar][:], recip[cpar][:], AF.Exp, scale=-1.0)
                        rdy = kb.inc("act", ins)
                        if dbg and s == 0 and ((dbg == "b") == is_b) and qc == 3 and h == 0:
                            kb.wait("ve", "act", rdy)
                            kb.inc("ve", VE.tensor_copy(dbg_rec[:], recip[cpar][:]))
                        if not DEFER_N2:
                            emit_n2(h, cpar, rdy)
                        else:
                            if pend is not None:
                                emit_n2(*pend)
                            pend = (h, cpar, rdy)
                    if pend is not None:
                        emit_n2(*pend)
                    pend = None
                    kb.wait("te", "ve", kb.cnt["ve"])
                    kb.wait("te", "dsp", kb.cnt["dsp"])
                    kb.wait("te", "act", psf_free)
                    for tt in range(4):
                        ti = qc * 4 + tt
                        kb.wait("te", "ve", o_free)
                        for pr in range(NP):
                            for eh in range(2):
                                ins = TE.matmul(
                                    ps_o[:, eh * 512:(eh + 1) * 512],
                                    ctxn[:, pr * QCH + tt * 128: pr * QCH + (tt + 1) * 128],
                                    wo_sb[:, pr * 1024 + eh * 512: pr * 1024 + (eh + 1) * 512],
                                    start=(pr == 0), stop=(pr == NP - 1 and not is_b))
                                if is_b and pr == NP - 1:
                                    ins = TE.matmul(ps_o[:, eh * 512:(eh + 1) * 512],
                                                    ones_row[:], borow[:, eh * 512:(eh + 1) * 512],
                                                    start=False, stop=True)
                        mm_o = kb.inc("te", ins)
                        kb.wait("ve", "te", mm_o)
                        xt = x_sb[:, ti * D:(ti + 1) * D]
                        ins = VE.tensor_add(xt, ps_o[:], xt)
                        o_free = kb.inc("ve", ins)
                    ctxn_rd = kb.cnt["te"]
                    if dbg and s == 0 and ((dbg == "b") == is_b) and qc == 3:
                        kb.join(("sp",))
                        kb.dma("dsp", dbg_outs["d_ctxn"], ctxn[:])
                        kb.dma("dsp", dbg_outs["d_p15"], dbg_p15[:])
                        kb.dma("dsp", dbg_outs["d_bias2"], bias2[:])
                        kb.dma("dsp", dbg_outs["d_s15"], dbg_s15[:])
                        kb.dma("dsp", dbg_outs["d_rec"], dbg_rec[:])
                        kb.wait("sp", "dsp", kb.cnt["dsp"])
                kb.join()

        def _unused():
            pass

    def emit_ffn(s):
        with ExitStack() as fsc:
            fsb = lambda n, sh, dt: fsc.enter_context(nc.sbuf_tensor(uname(n), sh, dt))
            ln2t = fsb("ln2t", [128, EC * T], BF16)
            mk_bd = fsb("mk_bd2", [128, 32 * 128], BF16)
            up_bd = fsb("up_bd2", [128, 64 * 128], BF16)
            kb.dma("dsp", mk_bd[:], rails_dram[:, 0:32 * 128])
            c_rails = kb.dma("dsp", up_bd[:], rails_dram[:, 32 * 128:96 * 128])
            with ExitStack() as psc:
                sc = ln_scratch(psc, "_ff")
                ps_t = [psc.enter_context(nc.psum_tensor(uname(f"ps_t2{i}"), [128, 512], BF16)) for i in range(2)]
                ps_sq = psc.enter_context(nc.psum_tensor(uname("ps_sq2"), [128, 2048], F32))
                emit_ln(ln2t, sc, ps_t, [ps_sq[:, 0:512], ps_sq[:, 512:1024],
                                         ps_sq[:, 1024:1536], ps_sq[:, 1536:2048]])
                kb.join()
            wct = [fsb(f"wct{i}", [128, SUPC * 1024], BF16) for i in range(2)]
            hsup = fsb("hsup", [128, SUPC * QCH], BF16)
            ysb = [fsb(f"ysb{i}", [128, QCH], BF16) for i in range(2)]
            ps_y = [fsc.enter_context(nc.psum_tensor(uname(f"ps_y{i}"), [128, QCH], F32)) for i in range(2)]
            ps_z = [fsc.enter_context(nc.psum_tensor(uname(f"ps_z{i}"), [128, 1024], F32)) for i in range(2)]
            ps_o = fsc.enter_context(nc.psum_tensor(uname("ps_o2"), [128, 1024], F32))

            wct_rdy = [0, 0]
            wct_free = [0, 0]
            wct_rdy[0] = kb.dma("dsp", wct[0][:].rearrange("p (ch e) -> p ch e", ch=SUPC),
                                wct_dram[0:SUPF, :].rearrange("(ch p) e -> p ch e", p=128))
            if dbg:
                kb.join(("sp",))
                kb.dma("dsp", dbg_outs["d_wct"], wct_dram[:])
                kb.dma("dsp", dbg_outs["d_ln2t"], ln2t[:])
                kb.dma("dsp", dbg_outs["d_mkbd"], mk_bd[:])
                kb.dma("dsp", dbg_outs["d_upbd"], up_bd[:])
                kb.wait("sp", "dsp", kb.cnt["dsp"])
                kb.join()
            ysb_free = [0, 0]
            psy_free = [0, 0]
            z_free = [0, 0]
            o_free = 0
            kb.wait("te", "dsp", c_rails)
            for sup in range(NSUP):
                wpar = sup % 2
                if sup + 1 < NSUP:
                    npar = (sup + 1) % 2
                    kb.wait("sp", "te", wct_free[npar])
                    kb.wait("sp", "dsp", kb.cnt["dsp"])
                    wct_rdy[npar] = kb.dma("dsp", wct[npar][:].rearrange("p (ch e) -> p ch e", ch=SUPC),
                                           wct_dram[(sup + 1) * SUPF:(sup + 2) * SUPF, :].rearrange(
                                               "(ch p) e -> p ch e", p=128))
                h = sup // 2
                gs = [4 * (sup % 2) + i for i in range(4)]
                kb.wait("te", "dsp", wct_rdy[wpar])
                gelu_mark = [0] * SUPC
                for t5 in range(4):
                    def emit_y(gi):
                        g = gs[gi]
                        ypar = gi % 2
                        kb.wait("te", "ve", psy_free[ypar])
                        ins = TE.matmul(ps_y[ypar][:],
                                        mk_bd[:, (h * 8 + g) * 128:(h * 8 + g + 1) * 128],
                                        ln2t[:, g * T + t5 * QCH: g * T + (t5 + 1) * QCH],
                                        start=True, stop=True)
                        return kb.inc("te", ins)
                    mm_ys = [emit_y(0)]
                    for gi, g in enumerate(gs):
                        ypar = gi % 2
                        kb.wait("ve", "te", mm_ys[gi])
                        kb.wait("ve", "te", ysb_free[ypar])
                        ins = VE.tensor_copy(ysb[ypar][:], ps_y[ypar][:])
                        psy_free[ypar] = kb.inc("ve", ins)
                        if gi + 1 < 4:
                            mm_ys.append(emit_y(gi + 1))
                        kb.wait("te", "ve", psy_free[ypar])
                        for zh in range(2):
                            zpar = zh
                            kb.wait("te", "act", z_free[zpar])
                            for half in range(2):
                                ch4 = zh * 2 + half
                                a = ch4 // 2
                                slot = h * 16 + g * 2 + (ch4 % 2)
                                ins = TE.matmul(ps_z[zpar][:, half * QCH:(half + 1) * QCH],
                                                up_bd[64 * a:64 * a + 64, slot * 128:(slot + 1) * 128],
                                                ysb[ypar][64 * a:64 * a + 64, :],
                                                start=True, stop=True)
                            mm_z = kb.inc("te", ins)
                            kb.wait("act", "te", mm_z)
                            chl = gi * 4 + zh * 2
                            ins = ACT.activation(hsup[:, chl * QCH:(chl + 2) * QCH], ps_z[zpar][:], AF.Gelu)
                            z_free[zpar] = kb.inc("act", ins)
                            gelu_mark[chl] = gelu_mark[chl + 1] = z_free[zpar]
                        ysb_free[ypar] = kb.cnt["te"]
                    if dbg and sup == 0 and t5 == 0:
                        kb.join(("sp",))
                        kb.dma("dsp", dbg_outs["d_hsup"], hsup[:])
                        kb.wait("sp", "dsp", kb.cnt["dsp"])
                        kb.join()
                    last_w = 0
                    for tt in range(4):
                        ti = t5 * 4 + tt
                        kb.wait("te", "ve", o_free)
                        for chl in range(SUPC):
                            if gelu_mark[chl] > last_w:
                                last_w = gelu_mark[chl]
                                kb.wait("te", "act", last_w)
                            for eh in range(2):
                                ins = TE.matmul(
                                    ps_o[:, eh * 512:(eh + 1) * 512],
                                    hsup[:, chl * QCH + tt * 128: chl * QCH + (tt + 1) * 128],
                                    wct[wpar][:, chl * 1024 + eh * 512: chl * 1024 + (eh + 1) * 512],
                                    start=(chl == 0), stop=(chl == SUPC - 1))
                        mm_o = kb.inc("te", ins)
                        kb.wait("ve", "te", mm_o)
                        xt = x_sb[:, ti * D:(ti + 1) * D]
                        ins = VE.tensor_add(xt, ps_o[:], xt)
                        o_free = kb.inc("ve", ins)
                wct_free[wpar] = kb.cnt["te"]
            kb.join()

    for s in range(n_seq):
        c_x = kb.dma("dsp", x_sb[:].rearrange("p (ti e) -> p ti e", ti=TTILES),
                      x_in[s].rearrange("(ti p) e -> p ti e", p=128))
        for e in ("te", "ve", "act", "pool"):
            kb.wait(e, "dsp", c_x)
        if "a" in stages:
            emit_attn(s, "a")
        if "f1" in stages:
            emit_ffn(s)
        if "b" in stages:
            emit_attn(s, "b")
        if "f2" in stages:
            emit_ffn(s)
        kb.join()
        kb.fence("ve", VE.tensor_copy(x_sb[:, 0:8], x_sb[:, 0:8]))
        kb.join(("sp",))
        kb.dma("dsp", x_out[s].rearrange("(ti p) e -> p ti e", p=128),
               x_sb[:].rearrange("p (ti e) -> p ti e", ti=TTILES))
    kb.wait("sp", "dsp", kb.cnt["dsp"])
    ctx.close()
    return nc


def kernel(**inputs):
    x = inputs["L0_attn_a_x"]
    nc = build_program()
    in_maps = []
    for c in range(NCORES):
        m = {"L0_attn_a_x": np.ascontiguousarray(x[SPC * c:SPC * (c + 1)])}
        for k, v in inputs.items():
            if k != "L0_attn_a_x":
                m[k] = v
        in_maps.append(m)
    res = run_bass_kernel_spmd(nc, in_maps, core_ids=list(range(NCORES)))
    return np.concatenate([res.results[c]["out"] for c in range(NCORES)], axis=0)
